# version 03
# baseline (speedup 1.0000x reference)
import numpy as np
import ml_dtypes
from contextlib import ExitStack

import concourse.bass as bass
import concourse.mybir as mybir
from concourse.bass_utils import run_bass_kernel_spmd

F32 = mybir.dt.float32
BF16 = mybir.dt.bfloat16
I32 = mybir.dt.int32
AX = mybir.AxisListType.X
OP = mybir.AluOpType
AF = mybir.ActivationFunctionType

P = 128
C = 4096
NCT = C // P
DQ = 1024
NDQ = DQ // P
S = 4096
NST = S // P
T = 1024
NTT = T // P
II = 11008
NIT = II // P
NGU = 32
NGD = 86
EPS = 1e-5
SCL = 1.0 / 32.0

ENGS = ("sync", "tensor", "vector", "scalar", "gpsimd")


class Em:

    def __init__(self, who, e, cnt):
        self.who = who
        self.e = e
        self.cnt = cnt

    def do(self, eng, fn, inc=None):
        if inc is not None:
            sem, n = inc
            if self.who == eng:
                fn(self.e).then_inc(sem, n)
            self.cnt[id(sem)] = self.cnt.get(id(sem), 0) + n
        elif self.who == eng:
            fn(self.e)

    def wait(self, eng, sem, val):
        if self.who == eng and val > 0:
            self.e.wait_ge(sem, val)

    def at(self, sem):
        return self.cnt.get(id(sem), 0)

    def dma(self, eng, fn, sem):
        self.do(eng, fn, inc=(sem, 16))
        return (sem, self.at(sem))

    def wt(self, eng, mark):
        if mark is not None:
            self.wait(eng, mark[0], mark[1])


def build_program():
    nc = bass.Bass("TRN2", target_bir_lowering=False)

    xb = nc.dram_tensor("xb", [S, C], F32, kind="ExternalInput").ap()
    wqkv = nc.dram_tensor("wqkv", [C, 3 * DQ], F32, kind="ExternalInput").ap()
    bq_c = nc.dram_tensor("bq_c", [P, 24], F32, kind="ExternalInput").ap()
    bq_r = nc.dram_tensor("bq_r", [1, 3 * DQ], F32, kind="ExternalInput").ap()
    wo = nc.dram_tensor("wo", [DQ, C], F32, kind="ExternalInput").ap()
    bo_r = nc.dram_tensor("bo_r", [1, C], F32, kind="ExternalInput").ap()
    qtg = nc.dram_tensor("qtg", [II, C], I32, kind="ExternalInput").ap()
    stg = nc.dram_tensor("stg", [II, NGU], F32, kind="ExternalInput").ap()
    ztg = nc.dram_tensor("ztg", [II, NGU], F32, kind="ExternalInput").ap()
    qtu = nc.dram_tensor("qtu", [II, C], I32, kind="ExternalInput").ap()
    stu = nc.dram_tensor("stu", [II, NGU], F32, kind="ExternalInput").ap()
    ztu = nc.dram_tensor("ztu", [II, NGU], F32, kind="ExternalInput").ap()
    qtd = nc.dram_tensor("qtd", [C, II], I32, kind="ExternalInput").ap()
    std = nc.dram_tensor("std", [C, NGD], F32, kind="ExternalInput").ap()
    ztd = nc.dram_tensor("ztd", [C, NGD], F32, kind="ExternalInput").ap()
    ident_d = nc.dram_tensor("ident", [P, P], BF16, kind="ExternalInput").ap()
    ones_d = nc.dram_tensor("ones", [1, P], BF16, kind="ExternalInput").ap()
    out_d = nc.dram_tensor("out", [T, C], F32, kind="ExternalOutput").ap()

    xnt_h = nc.dram_tensor("xnt_h", [NCT, P, S], BF16, kind="Internal").ap()
    v_h = nc.dram_tensor("v_h", [NST, P, DQ], BF16, kind="Internal").ap()
    sat_h = nc.dram_tensor("sat_h", [T, C], BF16, kind="Internal").ap()
    h_h = nc.dram_tensor("h_h", [4, P, NIT * 256], BF16, kind="Internal").ap()

    base = {}

    with ExitStack() as root:
        R1 = root.enter_context(nc.sbuf_tensor("R1", [P, 44032], BF16))
        R2 = root.enter_context(nc.sbuf_tensor("R2", [P, 44032], BF16))
        ident = root.enter_context(nc.sbuf_tensor("ident_sb", [P, P], BF16))
        ones = root.enter_context(nc.sbuf_tensor("ones_sb", [1, P], BF16))
        bq_sb = root.enter_context(nc.sbuf_tensor("bq_sb", [P, 24], F32))

        ps_sc = root.enter_context(nc.psum_tensor("ps_sc", [P, 2048], F32))
        ps_ta = root.enter_context(nc.psum_tensor("ps_ta", [P, 1024], BF16))
        ps_tb = root.enter_context(nc.psum_tensor("ps_tb", [P, 1024], BF16))
        ps_aa = root.enter_context(nc.psum_tensor("ps_aa", [P, 512], F32))
        ps_ab = root.enter_context(nc.psum_tensor("ps_ab", [P, 512], F32))

        def sem(name):
            return root.enter_context(nc.semaphore(name))

        eP, eV, eA, BAR = sem("eP"), sem("eV"), sem("eA"), sem("BAR")
        dma_sems = []

        def dsem(name, eng):
            s = sem(name)
            dma_sems.append((s, eng))
            return s

        qL = dsem("qL", "sync")
        qG = dsem("qG", "gpsimd")
        aX = [dsem(f"aX{i}", "sync") for i in range(2)]
        sS = [dsem(f"sS{i}", "sync") for i in range(2)]
        bX = [dsem(f"bX{i}", "sync") for i in range(2)]
        bV = [dsem(f"bV{i}", "sync") for i in range(2)]
        dS = [dsem(f"dS{i}", "sync") for i in range(2)]
        eZ = [dsem(f"eZ{i}", "sync") for i in range(2)]
        eH = [dsem(f"eH{i}", "sync") for i in range(2)]
        eC = [dsem(f"eC{i}", "gpsimd") for i in range(2)]
        fH = [dsem(f"fH{i}", "sync") for i in range(2)]
        fC = [dsem(f"fC{i}", "sync") for i in range(2)]
        fO = [dsem(f"fO{i}", "sync") for i in range(2)]

        def barrier(em):
            tgt = em.at(BAR) + 5
            for eng in ENGS:
                for s, owner in dma_sems:
                    if owner == eng:
                        em.wait(eng, s, em.at(s))
                em.do(eng, lambda e: e.nop(), inc=(BAR, 1))
            for eng in ENGS:
                em.wait(eng, BAR, tgt)

        def run_phases(phases, alloc=None):
            with ExitStack() as ph:
                bufs = alloc(ph) if alloc is not None else None
                for nm, fn in phases:
                    with nc.Block(nm) as bl:
                        for eng in ENGS:
                            def mk(eng=eng, fn=fn):
                                def body(e):
                                    em = Em(eng, e, dict(base))
                                    fn(em, bufs)
                                    barrier(em)
                                return body
                            getattr(bl, eng)(mk())
                    em = Em(None, None, base)
                    fn(em, bufs)
                    barrier(em)

        def ph_init(em, _):
            em.dma("sync", lambda e: e.dma_start(out=ident[:], in_=ident_d), qL)
            em.dma("sync", lambda e: e.dma_start(out=ones[:], in_=ones_d), qL)
            em.dma("sync", lambda e: e.dma_start(out=bq_sb[:], in_=bq_c), qL)

        run_phases([("init", ph_init)])

        def alloc_A(ph):
            b = {}
            b["xin"] = [ph.enter_context(nc.sbuf_tensor(f"xin{i}", [P, C], F32)) for i in range(2)]
            b["stats"] = ph.enter_context(nc.sbuf_tensor("stats", [P, 48], F32))
            b["st1"] = ph.enter_context(nc.sbuf_tensor("st1", [P, 8], F32))
            return b

        def ph_A(em, b):
            xin, st1, stats = b["xin"], b["st1"], b["stats"]
            xnbf = R2[:, 0:4096]
            stage = [R2[:, 4096:5120], R2[:, 5120:6144]]
            agg = st1[:, 0:2]
            var = st1[:, 1:2]
            nmean, sig, rstd = (st1[:, i:i + 1] for i in range(2, 5))
            lx = [em.dma("sync", lambda e: e.dma_start(out=xin[0][:], in_=xb[0:P, :]), aX[0])]
            v_done = []
            s_evac = []
            ss = []
            p_done = []
            for st in range(NST):
                if st + 1 < NST:
                    if st >= 1:
                        em.wait("sync", eV, v_done[st - 1])
                    lx.append(em.dma("sync", lambda e, st=st: e.dma_start(
                        out=xin[(st + 1) % 2][:], in_=xb[(st + 1) * P:(st + 2) * P, :]), aX[(st + 1) % 2]))
                cur = xin[st % 2]
                em.wt("vector", lx[st])
                for i in range(8):
                    em.do("vector", lambda e, cur=cur, i=i: e.bn_stats(
                        out=stats[:, 6 * i:6 * (i + 1)], in_=cur[:, 512 * i:512 * (i + 1)]),
                        inc=(eV, 1) if i == 7 else None)
                em.wait("vector", eV, em.at(eV))
                em.do("vector", lambda e: e.bn_aggr(out=agg, in_=stats[:]), inc=(eV, 1))
                em.wait("vector", eV, em.at(eV))
                em.do("vector", lambda e: e.tensor_scalar(
                    out=nmean, in0=agg[:, 0:1], scalar1=-1.0, scalar2=None, op0=OP.mult))
                em.do("vector", lambda e: e.tensor_scalar(
                    out=var, in0=var, scalar1=EPS, scalar2=None, op0=OP.add), inc=(eV, 1))
                var_done = em.at(eV)
                em.wait("scalar", eV, var_done)
                em.do("scalar", lambda e: e.activation(out=sig, in_=var, func=AF.Sqrt), inc=(eA, 1))
                sig_done = em.at(eA)
                em.wait("vector", eA, sig_done)
                em.do("vector", lambda e: e.reciprocal(out=rstd, in_=sig), inc=(eV, 1))
                em.wait("vector", eV, em.at(eV))
                if st >= 1:
                    em.wait("vector", eP, p_done[4 * (st - 1) + 3])
                em.do("vector", lambda e, cur=cur: e.tensor_scalar(
                    out=xnbf, in0=cur[:], scalar1=nmean, scalar2=rstd, op0=OP.add, op1=OP.mult), inc=(eV, 1))
                v_done.append(em.at(eV))
                em.wait("tensor", eV, v_done[st])
                for k in range(4):
                    gk = st * 4 + k
                    bank = ps_ta if k % 2 == 0 else ps_tb
                    if gk >= 2:
                        em.wait("tensor", eA, s_evac[gk - 2])
                    for j in range(8):
                        ct = k * 8 + j
                        em.do("tensor", lambda e, ct=ct, bank=bank, j=j: e.transpose(
                            bank[:, j * 128:(j + 1) * 128], xnbf[:, ct * 128:(ct + 1) * 128], ident[:]),
                            inc=(eP, 1) if j == 7 else None)
                    p_done.append(em.at(eP))
                    em.wait("scalar", eP, p_done[gk])
                    if gk >= 2:
                        em.wt("scalar", ss[gk - 2])
                    em.do("scalar", lambda e, bank=bank, k=k: e.copy(out=stage[k % 2], in_=bank[:]), inc=(eA, 1))
                    s_evac.append(em.at(eA))
                    em.wait("sync", eA, s_evac[gk])
                    ss.append(em.dma("sync", lambda e, st=st, k=k: e.dma_start(
                        out=xnt_h[k * 8:(k + 1) * 8, :, st * P:(st + 1) * P].rearrange("a p s -> p a s"),
                        in_=stage[k % 2].rearrange("p (a s) -> p a s", a=8)), sS[k % 2]))

        run_phases([("lnT", ph_A)], alloc_A)

        def load_wproj(em, col0, col1):
            mk = None
            w = col1 - col0
            for ct in range(NCT):
                mk = em.dma("gpsimd", lambda e, ct=ct, col0=col0, col1=col1, w=w: e.dma_start(
                    out=R1[:, ct * w:(ct + 1) * w],
                    in_=wqkv[ct * P:(ct + 1) * P, col0:col1]), qG)
            return mk

        def load_xslab(em, buf, sb):
            return em.dma("sync", lambda e, buf=buf, sb=sb: e.dma_start(
                out=buf.rearrange("p (a s) -> p a s", a=NCT),
                in_=xnt_h[:, :, sb * 256:(sb + 1) * 256].rearrange("a p s -> p a s")), bX[sb % 2])

        def ph_Bv(em, b):
            xslab = [R2[:, 0:8192], R2[:, 8192:16384]]
            vstg = [R2[:, 16384:17408], R2[:, 17408:18432]]
            bvrow = R2[0:1, 18432:19456]
            em.dma("gpsimd", lambda e, bvrow=bvrow: e.dma_start(out=bvrow, in_=bq_r[:, 2 * DQ:3 * DQ]), qG)
            wv_mark = load_wproj(em, 2 * DQ, 3 * DQ)
            lx = [load_xslab(em, xslab[0], 0)]
            em.wt("tensor", wv_mark)
            v_evac = []
            mm_done = []
            vs = []
            for sb in range(16):
                if sb + 1 < 16:
                    if sb >= 1:
                        em.wait("sync", eP, mm_done[2 * (sb - 1) + 1])
                    lx.append(load_xslab(em, xslab[(sb + 1) % 2], sb + 1))
                em.wt("tensor", lx[sb])
                for j in range(2):
                    sc = sb * 2 + j
                    if sc >= 1:
                        em.wait("tensor", eV, v_evac[sc - 1])
                    for ct in range(NCT):
                        em.do("tensor", lambda e, ct=ct, sb=sb, j=j: e.matmul(
                            ps_aa[:], xslab[sb % 2][:, ct * 256 + j * 128: ct * 256 + (j + 1) * 128],
                            R1[:, ct * DQ: ct * DQ + 512], start=(ct == 0), stop=False))
                        em.do("tensor", lambda e, ct=ct, sb=sb, j=j: e.matmul(
                            ps_ab[:], xslab[sb % 2][:, ct * 256 + j * 128: ct * 256 + (j + 1) * 128],
                            R1[:, ct * DQ + 512:(ct + 1) * DQ], start=(ct == 0), stop=False))
                    em.do("tensor", lambda e, bvrow=bvrow: e.matmul(
                        ps_aa[:], ones[:], bvrow[:, 0:512], start=False, stop=True))
                    em.do("tensor", lambda e, bvrow=bvrow: e.matmul(
                        ps_ab[:], ones[:], bvrow[:, 512:1024], start=False, stop=True), inc=(eP, 1))
                    mm_done.append(em.at(eP))
                    em.wait("vector", eP, mm_done[sc])
                    if sc >= 2:
                        em.wt("vector", vs[sc - 2])
                    em.do("vector", lambda e, sc=sc: e.tensor_copy(out=vstg[sc % 2][:, 0:512], in_=ps_aa[:]))
                    em.do("vector", lambda e, sc=sc: e.tensor_copy(out=vstg[sc % 2][:, 512:1024], in_=ps_ab[:]), inc=(eV, 1))
                    v_evac.append(em.at(eV))
                    em.wait("sync", eV, v_evac[sc])
                    vs.append(em.dma("sync", lambda e, sc=sc: e.dma_start(
                        out=v_h[sc], in_=vstg[sc % 2]), bV[sc % 2]))

        run_phases([("projV", ph_Bv)])

        def alloc_Bk(ph):
            b = {}
            b["xslab"] = [ph.enter_context(nc.sbuf_tensor(f"kxsl{i}", [P, NCT * 256], BF16)) for i in range(2)]
            return b

        def ph_Bk(em, b):
            xslab = b["xslab"]
            wk_mark = load_wproj(em, DQ, 2 * DQ)
            lx = [load_xslab(em, xslab[0][:], 0)]
            em.wt("tensor", wk_mark)
            mm_done = []
            k_evac = []
            gidx = 0
            for sb in range(16):
                if sb + 1 < 16:
                    if sb >= 1:
                        em.wait("sync", eP, mm_done[8 * (sb - 1) + 7])
                    lx.append(load_xslab(em, xslab[(sb + 1) % 2][:], sb + 1))
                em.wt("tensor", lx[sb])
                for dk in range(NDQ):
                    ps = ps_aa if gidx % 2 == 0 else ps_ab
                    if gidx >= 2:
                        em.wait("tensor", eV, k_evac[gidx - 2])
                    for ct in range(NCT):
                        em.do("tensor", lambda e, ct=ct, dk=dk, sb=sb, ps=ps: e.matmul(
                            ps[:, 0:256], R1[:, ct * DQ + dk * P: ct * DQ + (dk + 1) * P],
                            xslab[sb % 2][:, ct * 256:(ct + 1) * 256],
                            start=(ct == 0), stop=(ct == NCT - 1)), inc=(eP, 1) if ct == NCT - 1 else None)
                    mm_done.append(em.at(eP))
                    em.wait("vector", eP, mm_done[gidx])
                    em.do("vector", lambda e, dk=dk, sb=sb, ps=ps: e.tensor_scalar(
                        out=R2[:, dk * S + sb * 256: dk * S + (sb + 1) * 256], in0=ps[:, 0:256],
                        scalar1=bq_sb[:, 8 + dk: 9 + dk], scalar2=None, op0=OP.add), inc=(eV, 1))
                    k_evac.append(em.at(eV))
                    gidx += 1

        run_phases([("projK", ph_Bk)], alloc_Bk)

        def alloc_C(ph):
            b = {}
            b["qT"] = ph.enter_context(nc.sbuf_tensor("qTb", [P, NDQ * T], BF16))
            b["Pb"] = ph.enter_context(nc.sbuf_tensor("Pbb", [P, S], BF16))
            b["at"] = ph.enter_context(nc.sbuf_tensor("atb", [P, DQ], BF16))
            b["sm"] = ph.enter_context(nc.sbuf_tensor("smb", [P, 16], F32))
            return b

        def ph_Bq(em, b):
            qT = b["qT"]
            xq = R2[:, 32768:40960]
            wq_mark = load_wproj(em, 0, DQ)
            mm_done = []
            q_evac = []
            gidx = 0
            for sb in range(4):
                if sb >= 1:
                    em.wait("sync", eP, mm_done[8 * (sb - 1) + 7])
                xq_mark = em.dma("sync", lambda e, sb=sb, xq=xq: e.dma_start(
                    out=xq.rearrange("p (a s) -> p a s", a=NCT),
                    in_=xnt_h[:, :, sb * 256:(sb + 1) * 256].rearrange("a p s -> p a s")), qL)
                em.wt("tensor", xq_mark)
                em.wt("tensor", wq_mark)
                for dk in range(NDQ):
                    ps = ps_aa if gidx % 2 == 0 else ps_ab
                    if gidx >= 2:
                        em.wait("tensor", eV, q_evac[gidx - 2])
                    for ct in range(NCT):
                        em.do("tensor", lambda e, ct=ct, dk=dk, ps=ps, xq=xq: e.matmul(
                            ps[:, 0:256], R1[:, ct * DQ + dk * P: ct * DQ + (dk + 1) * P],
                            xq[:, ct * 256:(ct + 1) * 256],
                            start=(ct == 0), stop=(ct == NCT - 1)), inc=(eP, 1) if ct == NCT - 1 else None)
                    mm_done.append(em.at(eP))
                    em.wait("vector", eP, mm_done[gidx])
                    em.do("vector", lambda e, dk=dk, sb=sb, ps=ps: e.tensor_scalar(
                        out=qT[:, dk * T + sb * 256: dk * T + (sb + 1) * 256], in0=ps[:, 0:256],
                        scalar1=bq_sb[:, dk: dk + 1], scalar2=None, op0=OP.add), inc=(eV, 1))
                    q_evac.append(em.at(eV))
                    gidx += 1

        def ph_C(em, b):
            qT, Pb, at, sm = b["qT"], b["Pb"], b["at"], b["sm"]
            attnT = R1[:, 32768:40960]
            m0, m1, mx, c0, c1, t0_, t1_, l_ = (sm[:, i:i + 1] for i in range(8))
            v_mark = em.dma("sync", lambda e: e.dma_start(
                out=R1[:, 0:NST * DQ].rearrange("p (a d) -> p a d", a=NST),
                in_=v_h.rearrange("a p d -> p a d")), qL)
            exp_done = []
            ptr_done = []
            pt_evac = []
            amm_done = []
            at_evac = []
            att_done = []
            attT_evac = []
            for qt in range(NTT):
                for half in range(2):
                    if qt or half:
                        em.wait("tensor", eA, exp_done[-1])
                    for sbl in range(4):
                        sb = half * 4 + sbl
                        for dk in range(NDQ):
                            em.do("tensor", lambda e, sbl=sbl, sb=sb, dk=dk, qt=qt: e.matmul(
                                ps_sc[:, sbl * 512:(sbl + 1) * 512],
                                qT[:, dk * T + qt * P: dk * T + (qt + 1) * P],
                                R2[:, dk * S + sb * 512: dk * S + (sb + 1) * 512],
                                start=(dk == 0), stop=(dk == NDQ - 1)),
                                inc=(eP, 1) if (dk == NDQ - 1 and sbl == 3) else None)
                    sc_done = em.at(eP)
                    em.wait("vector", eP, sc_done)
                    mdst = m0 if half == 0 else m1
                    nm = sm[:, 8 + half: 9 + half]
                    em.do("vector", lambda e, mdst=mdst: e.reduce_max(out=mdst, in_=ps_sc[:], axis=AX), inc=(eV, 1))
                    em.wait("vector", eV, em.at(eV))
                    em.do("vector", lambda e, mdst=mdst, nm=nm: e.tensor_scalar(
                        out=nm, in0=mdst, scalar1=-SCL, scalar2=None, op0=OP.mult), inc=(eV, 1))
                    nm_done = em.at(eV)
                    em.wait("scalar", eV, nm_done)
                    if qt >= 1 and half == 0:
                        em.wait("scalar", eP, amm_done[qt - 1])
                    em.do("scalar", lambda e, half=half, nm=nm: e.activation(
                        out=Pb[:, half * 2048:(half + 1) * 2048], in_=ps_sc[:],
                        func=AF.Exp, bias=nm, scale=SCL), inc=(eA, 1))
                    exp_done.append(em.at(eA))
                    em.wait("vector", eA, exp_done[-1])
                    sdst = t0_ if half == 0 else t1_
                    em.do("vector", lambda e, half=half, sdst=sdst: e.reduce_sum(
                        out=sdst, in_=Pb[:, half * 2048:(half + 1) * 2048], axis=AX), inc=(eV, 1))
                    em.wait("vector", eV, em.at(eV))
                em.do("vector", lambda e: e.tensor_tensor(out=mx, in0=m0, in1=m1, op=OP.max), inc=(eV, 1))
                em.wait("vector", eV, em.at(eV))
                em.do("vector", lambda e: e.tensor_tensor(out=c0, in0=m0, in1=mx, op=OP.subtract))
                em.do("vector", lambda e: e.tensor_tensor(out=c1, in0=m1, in1=mx, op=OP.subtract), inc=(eV, 1))
                dif_done = em.at(eV)
                em.wait("scalar", eV, dif_done)
                em.do("scalar", lambda e: e.activation(out=c0, in_=c0, func=AF.Exp, scale=SCL))
                em.do("scalar", lambda e: e.activation(out=c1, in_=c1, func=AF.Exp, scale=SCL), inc=(eA, 1))
                cc_done = em.at(eA)
                em.wait("vector", eA, cc_done)
                linv = sm[:, 10 + qt % 2:11 + qt % 2]
                em.do("vector", lambda e: e.tensor_scalar(out=t0_, in0=t0_, scalar1=c0, scalar2=None, op0=OP.mult))
                em.do("vector", lambda e: e.tensor_scalar(out=t1_, in0=t1_, scalar1=c1, scalar2=None, op0=OP.mult), inc=(eV, 1))
                em.wait("vector", eV, em.at(eV))
                em.do("vector", lambda e: e.tensor_tensor(out=l_, in0=t0_, in1=t1_, op=OP.add), inc=(eV, 1))
                em.wait("vector", eV, em.at(eV))
                em.do("vector", lambda e, linv=linv: e.reciprocal(out=linv, in_=l_), inc=(eV, 1))
                em.wait("vector", eV, em.at(eV))
                em.do("vector", lambda e: e.tensor_scalar(
                    out=Pb[:, 0:2048], in0=Pb[:, 0:2048], scalar1=c0, scalar2=None, op0=OP.mult))
                em.do("vector", lambda e: e.tensor_scalar(
                    out=Pb[:, 2048:4096], in0=Pb[:, 2048:4096], scalar1=c1, scalar2=None, op0=OP.mult), inc=(eV, 1))
                pfin = em.at(eV)
                em.wait("tensor", eV, pfin)
                if qt >= 1:
                    em.wait("tensor", eA, attT_evac[qt - 1])
                for k in range(4):
                    gk = qt * 4 + k
                    bank = ps_ta if k % 2 == 0 else ps_tb
                    if gk >= 2:
                        em.wait("tensor", eA, pt_evac[gk - 2])
                    for j in range(8):
                        sc = k * 8 + j
                        em.do("tensor", lambda e, sc=sc, j=j, bank=bank: e.transpose(
                            bank[:, j * 128:(j + 1) * 128], Pb[:, sc * 128:(sc + 1) * 128], ident[:]),
                            inc=(eP, 1) if j == 7 else None)
                    tb_done = em.at(eP)
                    em.wait("scalar", eP, tb_done)
                    em.do("scalar", lambda e, bank=bank, k=k: e.copy(
                        out=Pb[:, k * 1024:(k + 1) * 1024], in_=bank[:]), inc=(eA, 1))
                    pt_evac.append(em.at(eA))
                ptr_done.append(em.at(eP))
                em.wait("tensor", eA, pt_evac[-1])
                em.wt("tensor", v_mark)
                if qt >= 1:
                    em.wait("tensor", eV, at_evac[qt - 1])
                for sc in range(NST):
                    em.do("tensor", lambda e, sc=sc: e.matmul(
                        ps_aa[:], Pb[:, sc * P:(sc + 1) * P],
                        R1[:, sc * DQ: sc * DQ + 512], start=(sc == 0), stop=(sc == NST - 1)))
                    em.do("tensor", lambda e, sc=sc: e.matmul(
                        ps_ab[:], Pb[:, sc * P:(sc + 1) * P],
                        R1[:, sc * DQ + 512:(sc + 1) * DQ], start=(sc == 0), stop=(sc == NST - 1)),
                        inc=(eP, 1) if sc == NST - 1 else None)
                amm_done.append(em.at(eP))
                em.wait("vector", eP, amm_done[qt])
                em.do("vector", lambda e, linv=linv: e.tensor_scalar(
                    out=at[:, 0:512], in0=ps_aa[:], scalar1=linv, scalar2=None, op0=OP.mult))
                em.do("vector", lambda e, linv=linv: e.tensor_scalar(
                    out=at[:, 512:1024], in0=ps_ab[:], scalar1=linv, scalar2=None, op0=OP.mult), inc=(eV, 1))
                at_evac.append(em.at(eV))
                em.wait("tensor", eV, at_evac[qt])
                if qt >= 1:
                    em.wait("tensor", eA, attT_evac[qt - 1])
                for j in range(NDQ):
                    em.do("tensor", lambda e, j=j: e.transpose(
                        ps_ta[:, j * 128:(j + 1) * 128], at[:, j * 128:(j + 1) * 128], ident[:]),
                        inc=(eP, 1) if j == NDQ - 1 else None)
                att_done.append(em.at(eP))
                em.wait("scalar", eP, att_done[qt])
                em.do("scalar", lambda e, qt=qt, attnT=attnT: e.copy(
                    out=attnT.rearrange("p (a t) -> p a t", a=NDQ)[:, :, qt * P:(qt + 1) * P],
                    in_=ps_ta[:].rearrange("p (a t) -> p a t", a=NDQ)), inc=(eA, 1))
                attT_evac.append(em.at(eA))
            base_vals["amm_last"] = amm_done[-1]
            base_vals["exp_last"] = exp_done[-1]

        base_vals = {}
        run_phases([("projQ", ph_Bq), ("attnC", ph_C)], alloc_C)

        def ph_D(em, b):
            attnT = R1[:, 32768:40960]
            sat = R2[:, 40960:43008]
            bo_sb = R2[0:1, 36864:40960]
            em.wait("gpsimd", eP, base_vals["amm_last"])
            em.dma("gpsimd", lambda e, bo_sb=bo_sb: e.dma_start(out=bo_sb, in_=bo_r), qG)
            wo_mark = None
            for dvc in range(NDQ):
                wo_mark = em.dma("gpsimd", lambda e, dvc=dvc: e.dma_start(
                    out=R2[:, dvc * C:(dvc + 1) * C],
                    in_=wo[dvc * P:(dvc + 1) * P, :]), qG)
            em.wt("tensor", wo_mark)
            em.wait("tensor", eA, base_vals["exp_last"])
            ds = []
            evac_done = []
            for tt in range(NTT):
                for pas in range(2):
                    gk = tt * 2 + pas
                    if gk >= 1:
                        em.wait("tensor", eA, evac_done[gk - 1])
                    for cbl in range(4):
                        cb = pas * 4 + cbl
                        for dvc in range(NDQ):
                            em.do("tensor", lambda e, cbl=cbl, cb=cb, dvc=dvc, tt=tt: e.matmul(
                                ps_sc[:, cbl * 512:(cbl + 1) * 512],
                                attnT[:, dvc * T + tt * P: dvc * T + (tt + 1) * P],
                                R2[:, dvc * C + cb * 512: dvc * C + (cb + 1) * 512],
                                start=(dvc == 0), stop=False))
                        em.do("tensor", lambda e, cbl=cbl, cb=cb, bo_sb=bo_sb: e.matmul(
                            ps_sc[:, cbl * 512:(cbl + 1) * 512], ones[:],
                            bo_sb[:, cb * 512:(cb + 1) * 512], start=False, stop=True),
                            inc=(eP, 1) if cbl == 3 else None)
                    mm_done = em.at(eP)
                    em.wait("scalar", eP, mm_done)
                    if gk >= 1:
                        em.wt("scalar", ds[gk - 1])
                    em.do("scalar", lambda e, sat=sat: e.copy(out=sat, in_=ps_sc[:]), inc=(eA, 1))
                    evac_done.append(em.at(eA))
                    em.wait("sync", eA, evac_done[gk])
                    ds.append(em.dma("sync", lambda e, tt=tt, pas=pas, sat=sat: e.dma_start(
                        out=sat_h[tt * P:(tt + 1) * P, pas * 2048:(pas + 1) * 2048],
                        in_=sat), dS[gk % 2]))

        run_phases([("projO", ph_D)])

        def alloc_E(ph):
            b = {}
            b["szg"] = [ph.enter_context(nc.sbuf_tensor(f"szg{i}", [P, 64], F32)) for i in range(2)]
            b["szu"] = [ph.enter_context(nc.sbuf_tensor(f"szu{i}", [P, 64], F32)) for i in range(2)]
            b["h"] = [ph.enter_context(nc.sbuf_tensor(f"hh{i}", [P, 1024], BF16)) for i in range(2)]
            b["sg"] = ph.enter_context(nc.sbuf_tensor("sgg", [P, 1024], BF16))
            return b

        def ph_E(em, b):
            szg, szu, hbuf, sgb = b["szg"], b["szu"], b["h"], b["sg"]
            xnt_mark = None
            for ct in range(NCT):
                xnt_mark = em.dma("sync", lambda e, ct=ct: e.dma_start(
                    out=R1[:, ct * T:(ct + 1) * T], in_=xnt_h[ct, :, 0:T]), qL)
            QG0, QU0, GD, UD, WG0, WU0 = 0, 8192, 16384, 20480, 24576, 32768
            deq_done = []
            tr_done = []
            tr_evac = []
            mm_done = []
            silu_done = []
            h_done = []
            hs = []
            cast_marks = []
            sz_marks = []

            def emit_h(em, it):
                em.wait("vector", eA, silu_done[it])
                if it >= 2:
                    em.wt("vector", hs[it - 2])
                em.do("vector", lambda e, it=it: e.tensor_tensor(
                    out=hbuf[it % 2][:, 0:512], in0=sgb[:, 0:512], in1=ps_sc[:, 1024:1536], op=OP.mult))
                em.do("vector", lambda e, it=it: e.tensor_tensor(
                    out=hbuf[it % 2][:, 512:1024], in0=sgb[:, 512:1024], in1=ps_sc[:, 1536:2048], op=OP.mult), inc=(eV, 1))
                h_done.append(em.at(eV))
                em.wait("sync", eV, h_done[it])
                mk = None
                for tp in range(4):
                    mk = em.dma("sync", lambda e, it=it, tp=tp: e.dma_start(
                        out=h_h[tp, :, it * 256:(it + 1) * 256],
                        in_=hbuf[it % 2][:, tp * 256:(tp + 1) * 256]), eH[it % 2])
                hs.append(mk)

            for it in range(NIT):
                if it >= 2:
                    em.wait("sync", eV, deq_done[2 * (it - 2) + 1])
                em.dma("sync", lambda e, it=it: e.dma_start(
                    out=szg[it % 2][:, 0:32], in_=stg[it * P:(it + 1) * P, :]), eZ[it % 2])
                em.dma("sync", lambda e, it=it: e.dma_start(
                    out=szg[it % 2][:, 32:64], in_=ztg[it * P:(it + 1) * P, :]), eZ[it % 2])
                em.dma("sync", lambda e, it=it: e.dma_start(
                    out=szu[it % 2][:, 0:32], in_=stu[it * P:(it + 1) * P, :]), eZ[it % 2])
                sz_marks.append(em.dma("sync", lambda e, it=it: e.dma_start(
                    out=szu[it % 2][:, 32:64], in_=ztu[it * P:(it + 1) * P, :]), eZ[it % 2]))
                if it >= 2:
                    em.wait("gpsimd", eV, deq_done[2 * (it - 2) + 1])
                em.dma("gpsimd", lambda e, it=it: e.dma_start(
                    out=R2[:, QG0 + (it % 2) * C: QG0 + (it % 2 + 1) * C],
                    in_=qtg[it * P:(it + 1) * P, :]), eC[it % 2])
                cast_marks.append(em.dma("gpsimd", lambda e, it=it: e.dma_start(
                    out=R2[:, QU0 + (it % 2) * C: QU0 + (it % 2 + 1) * C],
                    in_=qtu[it * P:(it + 1) * P, :]), eC[it % 2]))
                em.wt("vector", cast_marks[it])
                em.wt("vector", sz_marks[it])
                if it >= 1:
                    em.wait("vector", eP, tr_done[8 * (it - 1) + 7])
                em.do("vector", lambda e, it=it: e.scalar_tensor_tensor(
                    out=szg[it % 2][:, 32:64], in0=szg[it % 2][:, 0:32], scalar=-8.0,
                    in1=szg[it % 2][:, 32:64], op0=OP.mult, op1=OP.add))
                em.do("vector", lambda e, it=it: e.scalar_tensor_tensor(
                    out=szu[it % 2][:, 32:64], in0=szu[it % 2][:, 0:32], scalar=-8.0,
                    in1=szu[it % 2][:, 32:64], op0=OP.mult, op1=OP.add), inc=(eV, 1))
                em.wait("vector", eV, em.at(eV))
                for src0, dst0, szb in ((QG0, GD, szg), (QU0, UD, szu)):
                    for g in range(NGU):
                        em.do("vector", lambda e, it=it, g=g, src0=src0, dst0=dst0, szb=szb: e.tensor_scalar(
                            out=R2[:, dst0 + g * P: dst0 + (g + 1) * P],
                            in0=R2[:, src0 + (it % 2) * C + g * P: src0 + (it % 2) * C + (g + 1) * P],
                            scalar1=szb[it % 2][:, g:g + 1],
                            scalar2=szb[it % 2][:, 32 + g:33 + g],
                            op0=OP.mult, op1=OP.add), inc=(eV, 1) if g == NGU - 1 else None)
                    deq_done.append(em.at(eV))
                if it >= 1:
                    emit_h(em, it - 1)
                for k in range(8):
                    mat_off = GD if k < 4 else UD
                    src_deq = deq_done[2 * it] if k < 4 else deq_done[2 * it + 1]
                    em.wait("tensor", eV, src_deq)
                    gk = it * 8 + k
                    bank = ps_ta if k % 2 == 0 else ps_tb
                    if gk >= 2:
                        em.wait("tensor", eA, tr_evac[gk - 2])
                    for j in range(8):
                        ct = (k % 4) * 8 + j
                        em.do("tensor", lambda e, ct=ct, j=j, bank=bank, mat_off=mat_off: e.transpose(
                            bank[:, j * 128:(j + 1) * 128],
                            R2[:, mat_off + ct * P: mat_off + (ct + 1) * P], ident[:]),
                            inc=(eP, 1) if j == 7 else None)
                    tr_done.append(em.at(eP))
                    em.wait("scalar", eP, tr_done[gk])
                    wdst = WG0 if k < 4 else WU0
                    em.do("scalar", lambda e, bank=bank, k=k, it=it, wdst=wdst: e.copy(
                        out=R2[:, wdst + (it % 2) * C + (k % 4) * 1024: wdst + (it % 2) * C + (k % 4 + 1) * 1024],
                        in_=bank[:]), inc=(eA, 1))
                    tr_evac.append(em.at(eA))
                em.wait("tensor", eA, tr_evac[it * 8 + 7])
                em.wt("tensor", xnt_mark)
                if it >= 1:
                    em.wait("tensor", eA, silu_done[it - 1])
                    em.wait("tensor", eV, h_done[it - 1])
                for ct in range(NCT):
                    for mi, woff in ((0, WG0), (1, WU0)):
                        for th in range(2):
                            em.do("tensor", lambda e, ct=ct, mi=mi, woff=woff, th=th, it=it: e.matmul(
                                ps_sc[:, (mi * 2 + th) * 512:(mi * 2 + th + 1) * 512],
                                R2[:, woff + (it % 2) * C + ct * P: woff + (it % 2) * C + (ct + 1) * P],
                                R1[:, ct * T + th * 512: ct * T + (th + 1) * 512],
                                start=(ct == 0), stop=(ct == NCT - 1)),
                                inc=(eP, 1) if (ct == NCT - 1 and mi == 1 and th == 1) else None)
                mm_done.append(em.at(eP))
                em.wait("scalar", eP, mm_done[it])
                em.do("scalar", lambda e: e.activation(out=sgb[:, 0:512], in_=ps_sc[:, 0:512], func=AF.Silu))
                em.do("scalar", lambda e: e.activation(out=sgb[:, 512:1024], in_=ps_sc[:, 512:1024], func=AF.Silu), inc=(eA, 1))
                silu_done.append(em.at(eA))
            emit_h(em, NIT - 1)

        run_phases([("ffnGU", ph_E)], alloc_E)

        def alloc_F(ph):
            b = {}
            b["qd"] = ph.enter_context(nc.sbuf_tensor("qd", [P, II], BF16))
            b["szd"] = ph.enter_context(nc.sbuf_tensor("szd", [P, 2 * NGD], F32))
            b["g"] = [ph.enter_context(nc.sbuf_tensor(f"gg{i}", [P, 512], BF16)) for i in range(2)]
            b["sgd"] = ph.enter_context(nc.sbuf_tensor("sgdd", [P, 512], F32))
            b["satl"] = [ph.enter_context(nc.sbuf_tensor(f"satl{i}", [P, 512], BF16)) for i in range(2)]
            b["xres"] = [ph.enter_context(nc.sbuf_tensor(f"xres{i}", [P, 512], F32)) for i in range(2)]
            return b

        def ph_F(em, b):
            qd, szd, gbuf, sgd, satl, xres = (
                b["qd"], b["szd"], b["g"], b["sgd"], b["satl"], b["xres"])
            sd_ = szd[:, 0:NGD]
            z8d_ = szd[:, NGD:2 * NGD]
            hbufs = [R2[:, 0:22016], R2[:, 22016:44032]]
            deq_done = []
            tr_done = []
            tr_evac = []
            dn_done = []
            silu_done = []
            comb_done = []
            h_marks = []
            c_marks = []
            o_marks = []
            mmref = []
            for db in range(8):
                for dt in range(4):
                    d0 = db * 512 + dt * P
                    if deq_done:
                        em.wait("sync", eV, deq_done[-1])
                    em.dma("sync", lambda e, d0=d0: e.dma_start(out=sd_, in_=std[d0:d0 + P, :]), qL)
                    sz_mark = em.dma("sync", lambda e, d0=d0: e.dma_start(out=z8d_, in_=ztd[d0:d0 + P, :]), qL)
                    if tr_done:
                        em.wait("gpsimd", eP, tr_done[-1])
                    qd_mark = em.dma("gpsimd", lambda e, d0=d0: e.dma_start(
                        out=qd[:], in_=qtd[d0:d0 + P, :]), qG)
                    em.wt("vector", qd_mark)
                    em.wt("vector", sz_mark)
                    em.do("vector", lambda e: e.scalar_tensor_tensor(
                        out=z8d_, in0=sd_, scalar=-8.0, in1=z8d_, op0=OP.mult, op1=OP.add), inc=(eV, 1))
                    em.wait("vector", eV, em.at(eV))
                    for g in range(NGD):
                        em.do("vector", lambda e, g=g: e.tensor_scalar(
                            out=qd[:, g * P:(g + 1) * P], in0=qd[:, g * P:(g + 1) * P],
                            scalar1=sd_[:, g:g + 1], scalar2=z8d_[:, g:g + 1],
                            op0=OP.mult, op1=OP.add), inc=(eV, 1) if g == NGD - 1 else None)
                    deq_done.append(em.at(eV))
                    em.wait("tensor", eV, deq_done[-1])
                    nb = (NGD + 7) // 8
                    for k in range(nb):
                        j0 = k * 8
                        j1 = min(j0 + 8, NGD)
                        bank = ps_ta if len(tr_done) % 2 == 0 else ps_tb
                        if len(tr_evac) >= 2:
                            em.wait("tensor", eA, tr_evac[-2])
                        for j in range(j0, j1):
                            em.do("tensor", lambda e, j=j, j0=j0, bank=bank: e.transpose(
                                bank[:, (j - j0) * 128:(j - j0 + 1) * 128],
                                qd[:, j * P:(j + 1) * P], ident[:]),
                                inc=(eP, 1) if j == j1 - 1 else None)
                        tr_done.append(em.at(eP))
                        em.wait("scalar", eP, tr_done[-1])
                        w = (j1 - j0) * 128
                        em.do("scalar", lambda e, bank=bank, j0=j0, j1=j1, dt=dt, w=w: e.copy(
                            out=R1[:].rearrange("p (a t) -> p a t", a=NGD)[:, j0:j1, dt * P:(dt + 1) * P],
                            in_=bank[:, 0:w].rearrange("p (a t) -> p a t", a=j1 - j0)), inc=(eA, 1))
                        tr_evac.append(em.at(eA))
                wd_ready = em.at(eA)
                for tp in range(4):
                    gi = db * 4 + tp
                    hbf = hbufs[tp % 2]
                    if gi >= 2:
                        em.wait("sync", eP, dn_done[2 * (gi - 2) + 1])
                    h_marks.append(em.dma("sync", lambda e, tp=tp, hbf=hbf: e.dma_start(
                        out=hbf, in_=h_h[tp]), fH[tp % 2]))
                    em.wt("tensor", h_marks[-1])
                    if tp == 0:
                        em.wait("tensor", eA, wd_ready)
                    for ti in range(2):
                        tt = tp * 2 + ti
                        ps = ps_aa if tt % 2 == 0 else ps_ab
                        if 2 * gi + ti >= 2:
                            em.wait("tensor", eA, silu_done[2 * gi + ti - 2])
                        for a in range(NIT):
                            em.do("tensor", lambda e, a=a, ti=ti, ps=ps, hbf=hbf: e.matmul(
                                ps[:], hbf[:, a * 256 + ti * P: a * 256 + (ti + 1) * P],
                                R1[:, a * 512:(a + 1) * 512],
                                start=(a == 0), stop=(a == NIT - 1)), inc=(eP, 1) if a == NIT - 1 else None)
                        dn_done.append(em.at(eP))
                        if len(comb_done) >= 2:
                            em.wait("sync", eV, comb_done[-2])
                        if len(o_marks) >= 2:
                            em.wt("sync", o_marks[-2])
                        em.dma("sync", lambda e, tt=tt, db=db, ti=ti: e.dma_start(
                            out=satl[ti][:], in_=sat_h[tt * P:(tt + 1) * P, db * 512:(db + 1) * 512]), fC[ti])
                        c_marks.append(em.dma("sync", lambda e, tt=tt, db=db, ti=ti: e.dma_start(
                            out=xres[ti][:], in_=xb[tt * P:(tt + 1) * P, db * 512:(db + 1) * 512]), fC[ti]))
                        em.wait("scalar", eP, dn_done[-1])
                        em.do("scalar", lambda e, ti=ti, ps=ps: e.activation(
                            out=gbuf[ti][:], in_=ps[:], func=AF.Silu), inc=(eA, 1))
                        silu_done.append(em.at(eA))
                        em.wait("vector", eA, silu_done[-1])
                        em.wt("vector", c_marks[-1])
                        if len(o_marks) >= 2:
                            em.wt("vector", o_marks[-2])
                        em.do("vector", lambda e, ti=ti: e.tensor_tensor(
                            out=gbuf[ti][:], in0=satl[ti][:], in1=gbuf[ti][:], op=OP.mult), inc=(eV, 1))
                        em.wait("vector", eV, em.at(eV))
                        em.do("vector", lambda e, ti=ti: e.tensor_copy(out=sgd[:], in_=gbuf[ti][:]), inc=(eV, 1))
                        em.wait("vector", eV, em.at(eV))
                        em.do("vector", lambda e, ti=ti: e.tensor_tensor(
                            out=xres[ti][:], in0=xres[ti][:], in1=sgd[:], op=OP.add), inc=(eV, 1))
                        comb_done.append(em.at(eV))
                        em.wait("sync", eV, comb_done[-1])
                        o_marks.append(em.dma("sync", lambda e, tt=tt, db=db, ti=ti: e.dma_start(
                            out=out_d[tt * P:(tt + 1) * P, db * 512:(db + 1) * 512],
                            in_=xres[ti][:]), fO[ti]))
                mmref.append(em.at(eP))

        run_phases([("ffnDN", ph_F)], alloc_F)

    return nc


_NC_CACHE = {}


def kernel(L0_attn_x, L0_attn_Wqkv, L0_attn_bqkv, L0_attn_Wo, L0_attn_bo,
           L0_ffn_gate_q, L0_ffn_gate_s, L0_ffn_gate_z,
           L0_ffn_up_q, L0_ffn_up_s, L0_ffn_up_z,
           L0_ffn_down_q, L0_ffn_down_s, L0_ffn_down_z):
    x = np.asarray(L0_attn_x, dtype=np.float32)
    wqkv = np.ascontiguousarray(np.asarray(L0_attn_Wqkv, dtype=np.float32))
    bqkv = np.asarray(L0_attn_bqkv, dtype=np.float32)
    wo_np = np.ascontiguousarray(np.asarray(L0_attn_Wo, dtype=np.float32))
    bo = np.asarray(L0_attn_bo, dtype=np.float32)
    qtg = np.ascontiguousarray(np.asarray(L0_ffn_gate_q, dtype=np.int32).T)
    stg = np.ascontiguousarray(np.asarray(L0_ffn_gate_s, dtype=np.float32).T)
    ztg = np.ascontiguousarray(np.asarray(L0_ffn_gate_z, dtype=np.float32).T)
    qtu = np.ascontiguousarray(np.asarray(L0_ffn_up_q, dtype=np.int32).T)
    stu = np.ascontiguousarray(np.asarray(L0_ffn_up_s, dtype=np.float32).T)
    ztu = np.ascontiguousarray(np.asarray(L0_ffn_up_z, dtype=np.float32).T)
    qtd = np.ascontiguousarray(np.asarray(L0_ffn_down_q, dtype=np.int32).T)
    std = np.ascontiguousarray(np.asarray(L0_ffn_down_s, dtype=np.float32).T)
    ztd = np.ascontiguousarray(np.asarray(L0_ffn_down_z, dtype=np.float32).T)

    bq_c = np.ascontiguousarray(bqkv.reshape(24, P).T)
    bq_r = np.ascontiguousarray(bqkv.reshape(1, 3 * DQ))
    bo_r = np.ascontiguousarray(bo.reshape(1, C))
    ident = np.eye(P, dtype=ml_dtypes.bfloat16)
    ones = np.ones((1, P), dtype=ml_dtypes.bfloat16)

    if "nc" not in _NC_CACHE:
        _NC_CACHE["nc"] = build_program()
    nc = _NC_CACHE["nc"]

    in_maps = []
    for c in range(8):
        bi = c // 4
        t0 = (c % 4) * T
        xb_c = np.ascontiguousarray(np.roll(x[bi], -t0, axis=0))
        in_maps.append(dict(
            xb=xb_c, wqkv=wqkv, bq_c=bq_c, bq_r=bq_r, wo=wo_np, bo_r=bo_r,
            qtg=qtg, stg=stg, ztg=ztg, qtu=qtu, stu=stu, ztu=ztu,
            qtd=qtd, std=std, ztd=ztd, ident=ident, ones=ones,
        ))
    res = run_bass_kernel_spmd(nc, in_maps, core_ids=list(range(8)))
    out = np.empty((2, S, C), dtype=np.float32)
    for c in range(8):
        bi = c // 4
        t0 = (c % 4) * T
        out[bi, t0:t0 + T, :] = res.results[c]["out"]
    return out


if __name__ == "__main__":
    nc = build_program()
    n = sum(1 for _ in nc.all_instructions())
    print("built, instructions:", n)


# version 11
# speedup vs baseline: 1.3819x; 1.3819x over previous
import numpy as np
import ml_dtypes
from contextlib import ExitStack

import concourse.bass as bass
import concourse.mybir as mybir
from concourse.bass_utils import run_bass_kernel_spmd

F32 = mybir.dt.float32
BF16 = mybir.dt.bfloat16
I32 = mybir.dt.int32
AX = mybir.AxisListType.X
OP = mybir.AluOpType
AF = mybir.ActivationFunctionType

P = 128
C = 4096
NCT = C // P
DQ = 1024
NDQ = DQ // P
S = 4096
NST = S // P
T = 1024
NTT = T // P
II = 11008
NIT = II // P
NGU = 32
NGD = 86
EPS = 1e-5
SCL = 1.0 / 32.0

ENGS = ("sync", "tensor", "vector", "scalar", "gpsimd")


class Em:

    def __init__(self, who, e, cnt):
        self.who = who
        self.e = e
        self.cnt = cnt

    def do(self, eng, fn, inc=None):
        if inc is not None:
            sem, n = inc
            if self.who == eng:
                fn(self.e).then_inc(sem, n)
            self.cnt[id(sem)] = self.cnt.get(id(sem), 0) + n
        elif self.who == eng:
            fn(self.e)

    def wait(self, eng, sem, val):
        if self.who == eng and val > 0:
            self.e.wait_ge(sem, val)

    def at(self, sem):
        return self.cnt.get(id(sem), 0)

    def dma(self, eng, fn, sem):
        self.do(eng, fn, inc=(sem, 16))
        return (sem, self.at(sem))

    def wt(self, eng, mark):
        if mark is not None:
            self.wait(eng, mark[0], mark[1])


def build_program():
    nc = bass.Bass("TRN2", target_bir_lowering=False)

    xb = nc.dram_tensor("xb", [S, C], F32, kind="ExternalInput").ap()
    wqkv = nc.dram_tensor("wqkv", [C, 3 * DQ], F32, kind="ExternalInput").ap()
    bq_c = nc.dram_tensor("bq_c", [P, 24], F32, kind="ExternalInput").ap()
    bq_r = nc.dram_tensor("bq_r", [1, 3 * DQ], F32, kind="ExternalInput").ap()
    wo = nc.dram_tensor("wo", [DQ, C], F32, kind="ExternalInput").ap()
    bo_r = nc.dram_tensor("bo_r", [1, C], F32, kind="ExternalInput").ap()
    qtg = nc.dram_tensor("qtg", [II, C], I32, kind="ExternalInput").ap()
    stg = nc.dram_tensor("stg", [II, NGU], F32, kind="ExternalInput").ap()
    ztg = nc.dram_tensor("ztg", [II, NGU], F32, kind="ExternalInput").ap()
    qtu = nc.dram_tensor("qtu", [II, C], I32, kind="ExternalInput").ap()
    stu = nc.dram_tensor("stu", [II, NGU], F32, kind="ExternalInput").ap()
    ztu = nc.dram_tensor("ztu", [II, NGU], F32, kind="ExternalInput").ap()
    qtd = nc.dram_tensor("qtd", [C, II], I32, kind="ExternalInput").ap()
    std = nc.dram_tensor("std", [C, NGD], F32, kind="ExternalInput").ap()
    ztd = nc.dram_tensor("ztd", [C, NGD], F32, kind="ExternalInput").ap()
    ident_d = nc.dram_tensor("ident", [P, P], BF16, kind="ExternalInput").ap()
    ones_d = nc.dram_tensor("ones", [1, P], BF16, kind="ExternalInput").ap()
    out_d = nc.dram_tensor("out", [T, C], F32, kind="ExternalOutput").ap()

    xnt_h = nc.dram_tensor("xnt_h", [NCT, P, S], BF16, kind="Internal").ap()
    v_h = nc.dram_tensor("v_h", [NST, P, DQ], BF16, kind="Internal").ap()
    sat_h = nc.dram_tensor("sat_h", [T, C], BF16, kind="Internal").ap()
    h_h = nc.dram_tensor("h_h", [NTT, P, NIT * P], BF16, kind="Internal").ap()

    base = {}

    with ExitStack() as root:
        R1 = root.enter_context(nc.sbuf_tensor("R1", [P, 44032], BF16))
        R2 = root.enter_context(nc.sbuf_tensor("R2", [P, 44032], BF16))
        ident = root.enter_context(nc.sbuf_tensor("ident_sb", [P, P], BF16))
        ones = root.enter_context(nc.sbuf_tensor("ones_sb", [1, P], BF16))
        bq_sb = root.enter_context(nc.sbuf_tensor("bq_sb", [P, 24], F32))

        ps_sc = root.enter_context(nc.psum_tensor("ps_sc", [P, 2048], F32))
        ps_ta = root.enter_context(nc.psum_tensor("ps_ta", [P, 1024], BF16))
        ps_tb = root.enter_context(nc.psum_tensor("ps_tb", [P, 1024], BF16))
        ps_aa = root.enter_context(nc.psum_tensor("ps_aa", [P, 512], F32))
        ps_ab = root.enter_context(nc.psum_tensor("ps_ab", [P, 512], F32))

        def sem(name):
            return root.enter_context(nc.semaphore(name))

        eP, eV, eA, BAR = sem("eP"), sem("eV"), sem("eA"), sem("BAR")
        dma_sems = []

        def dsem(name, eng):
            s = sem(name)
            dma_sems.append((s, eng))
            return s

        qL = dsem("qL", "sync")
        qG = dsem("qG", "gpsimd")
        aX = [dsem(f"aX{i}", "sync") for i in range(2)]
        sS = [dsem(f"sS{i}", "sync") for i in range(4)]
        bX = [dsem(f"bX{i}", "sync") for i in range(2)]
        bV = [dsem(f"bV{i}", "sync") for i in range(2)]
        dS = [dsem(f"dS{i}", "sync") for i in range(2)]
        eZ = [dsem(f"eZ{i}", "sync") for i in range(2)]
        eH = [dsem(f"eH{i}", "sync") for i in range(2)]
        eC = [dsem(f"eC{i}", "gpsimd") for i in range(2)]
        fH = [dsem(f"fH{i}", "sync") for i in range(2)]
        fC = [dsem(f"fC{i}", "sync") for i in range(2)]
        fO = [dsem(f"fO{i}", "sync") for i in range(2)]
        fQ = [[dsem(f"fQ{i}_{p}", "gpsimd") for p in range(4)] for i in range(3)]
        fZ = [dsem(f"fZ{i}", "sync") for i in range(3)]
        fH2 = [dsem(f"fH2_{i}", "sync") for i in range(2)]

        def barrier(em):
            tgt = em.at(BAR) + 5
            for eng in ENGS:
                for s, owner in dma_sems:
                    if owner == eng:
                        em.wait(eng, s, em.at(s))
                em.do(eng, lambda e: e.nop(), inc=(BAR, 1))
            for eng in ENGS:
                em.wait(eng, BAR, tgt)

        def run_phases(phases, alloc=None):
            with ExitStack() as ph:
                bufs = alloc(ph) if alloc is not None else None
                for nm, fn in phases:
                    with nc.Block(nm) as bl:
                        for eng in ENGS:
                            def mk(eng=eng, fn=fn):
                                def body(e):
                                    em = Em(eng, e, dict(base))
                                    fn(em, bufs)
                                    barrier(em)
                                return body
                            getattr(bl, eng)(mk())
                    em = Em(None, None, base)
                    fn(em, bufs)
                    barrier(em)

        def ph_init(em, _):
            em.dma("sync", lambda e: e.dma_start(out=ident[:], in_=ident_d), qL)
            em.dma("sync", lambda e: e.dma_start(out=ones[:], in_=ones_d), qL)
            em.dma("sync", lambda e: e.dma_start(out=bq_sb[:], in_=bq_c), qL)

        run_phases([("init", ph_init)])

        def alloc_A(ph):
            b = {}
            b["xin"] = [ph.enter_context(nc.sbuf_tensor(f"xin{i}", [P, C], F32)) for i in range(2)]
            b["stats"] = ph.enter_context(nc.sbuf_tensor("stats", [P, 48], F32))
            b["st1"] = ph.enter_context(nc.sbuf_tensor("st1", [P, 8], F32))
            return b

        base_vals = {}

        def load_w(em, col0, col1, dst, doff):
            mks = []
            w = col1 - col0
            for ct in range(NCT):
                mks.append(em.dma("gpsimd", lambda e, ct=ct, col0=col0, col1=col1, w=w, dst=dst, doff=doff: e.dma_start(
                    out=dst[:, doff + ct * w: doff + (ct + 1) * w],
                    in_=wqkv[ct * P:(ct + 1) * P, col0:col1]), qG))
            return mks

        def load_xslab(em, buf, sb):
            return em.dma("sync", lambda e, buf=buf, sb=sb: e.dma_start(
                out=buf.rearrange("p (a s) -> p a s", a=NCT),
                in_=xnt_h[:, :, sb * 256:(sb + 1) * 256].rearrange("a p s -> p a s")), bX[sb % 2])

        def ph_AV(em, b):
            xin, st1, stats = b["xin"], b["st1"], b["stats"]
            xnbf = R2[:, 0:4096]
            slab = [R2[:, 4096:8192], R2[:, 8192:12288]]
            vstg = [R2[:, 12288:13312], R2[:, 13312:14336]]
            bvrow = R2[0:1, 14336:15360]
            agg = st1[:, 0:2]
            var = st1[:, 1:2]
            nmean, sig, rstd = (st1[:, i:i + 1] for i in range(2, 5))
            bv_mark = em.dma("gpsimd", lambda e, bvrow=bvrow: e.dma_start(out=bvrow, in_=bq_r[:, 2 * DQ:3 * DQ]), qG)
            wv_marks = load_w(em, 2 * DQ, 3 * DQ, R1, 0)
            for ct in range(28):
                base_vals["wkpre_mark"] = em.dma("gpsimd", lambda e, ct=ct: e.dma_start(
                    out=R2[:, 15360 + ct * 1024: 15360 + (ct + 1) * 1024],
                    in_=wqkv[ct * P:(ct + 1) * P, DQ:2 * DQ]), qG)
            lx = [em.dma("sync", lambda e: e.dma_start(out=xin[0][:], in_=xb[0:P, :]), aX[0])]
            v_done = []
            s_evac = []
            ss = []
            p_done = []
            vmm_done = []
            vevac = []
            vs = []

            def emit_vevac(em, t):
                em.wait("vector", eP, vmm_done[t])
                if t >= 2:
                    em.wt("vector", vs[t - 2])
                pb = (t % 2) * 1024
                em.do("vector", lambda e, t=t, pb=pb: e.tensor_copy(
                    out=vstg[t % 2][:, 0:512], in_=ps_sc[:, pb:pb + 512]))
                em.do("vector", lambda e, t=t, pb=pb: e.tensor_copy(
                    out=vstg[t % 2][:, 512:1024], in_=ps_sc[:, pb + 512:pb + 1024]), inc=(eV, 1))
                vevac.append(em.at(eV))
                em.wait("sync", eV, vevac[t])
                vs.append(em.dma("sync", lambda e, t=t: e.dma_start(
                    out=v_h[t], in_=vstg[t % 2]), bV[t % 2]))

            for st in range(NST):
                if st + 1 < NST:
                    if st >= 1:
                        em.wait("sync", eV, v_done[st - 1])
                    lx.append(em.dma("sync", lambda e, st=st: e.dma_start(
                        out=xin[(st + 1) % 2][:], in_=xb[(st + 1) * P:(st + 2) * P, :]), aX[(st + 1) % 2]))
                cur = xin[st % 2]
                em.wt("vector", lx[st])
                for i in range(8):
                    em.do("vector", lambda e, cur=cur, i=i: e.bn_stats(
                        out=stats[:, 6 * i:6 * (i + 1)], in_=cur[:, 512 * i:512 * (i + 1)]),
                        inc=(eV, 1) if i == 7 else None)
                em.wait("vector", eV, em.at(eV))
                em.do("vector", lambda e: e.bn_aggr(out=agg, in_=stats[:]), inc=(eV, 1))
                em.wait("vector", eV, em.at(eV))
                em.do("vector", lambda e: e.tensor_scalar(
                    out=nmean, in0=agg[:, 0:1], scalar1=-1.0, scalar2=None, op0=OP.mult))
                em.do("vector", lambda e: e.tensor_scalar(
                    out=var, in0=var, scalar1=EPS, scalar2=None, op0=OP.add), inc=(eV, 1))
                var_done = em.at(eV)
                em.wait("scalar", eV, var_done)
                em.do("scalar", lambda e: e.activation(out=sig, in_=var, func=AF.Sqrt), inc=(eA, 1))
                sig_done = em.at(eA)
                em.wait("vector", eA, sig_done)
                em.do("vector", lambda e: e.reciprocal(out=rstd, in_=sig), inc=(eV, 1))
                em.wait("vector", eV, em.at(eV))
                if st >= 1:
                    em.wait("vector", eP, p_done[4 * (st - 1) + 3])
                em.do("vector", lambda e, cur=cur: e.tensor_scalar(
                    out=xnbf, in0=cur[:], scalar1=nmean, scalar2=rstd, op0=OP.add, op1=OP.mult), inc=(eV, 1))
                v_done.append(em.at(eV))
                if st >= 1:
                    emit_vevac(em, st - 1)
                em.wait("tensor", eV, v_done[st])
                for k in range(4):
                    gk = st * 4 + k
                    bank = ps_ta if gk % 2 == 0 else ps_tb
                    if gk >= 2:
                        em.wait("tensor", eA, s_evac[gk - 2])
                    for j in range(8):
                        ct = k * 8 + j
                        em.do("tensor", lambda e, ct=ct, bank=bank, j=j: e.transpose(
                            bank[:, j * 128:(j + 1) * 128], xnbf[:, ct * 128:(ct + 1) * 128], ident[:]),
                            inc=(eP, 1) if j == 7 else None)
                    p_done.append(em.at(eP))
                    em.wait("scalar", eP, p_done[gk])
                    if st >= 2:
                        em.wt("scalar", ss[4 * (st - 2) + k])
                        if k == 0:
                            em.wait("scalar", eP, vmm_done[st - 2])
                    em.do("scalar", lambda e, bank=bank, st=st, k=k: e.copy(
                        out=slab[st % 2][:, k * 1024:(k + 1) * 1024], in_=bank[:]), inc=(eA, 1))
                    s_evac.append(em.at(eA))
                    em.wait("sync", eA, s_evac[gk])
                    ss.append(em.dma("sync", lambda e, st=st, k=k: e.dma_start(
                        out=xnt_h[k * 8:(k + 1) * 8, :, st * P:(st + 1) * P].rearrange("a p s -> p a s"),
                        in_=slab[st % 2][:, k * 1024:(k + 1) * 1024].rearrange("p (a s) -> p a s", a=8)), sS[k]))
                if st == 0:
                    em.wt("tensor", bv_mark)
                if st >= 2:
                    em.wait("tensor", eV, vevac[st - 2])
                pb = (st % 2) * 1024
                for ct in range(NCT):
                    if ct % 8 == 0:
                        em.wait("tensor", eA, s_evac[st * 4 + ct // 8])
                        if st == 0:
                            em.wt("tensor", wv_marks[min(ct + 7, NCT - 1)])
                    for dvh in range(2):
                        em.do("tensor", lambda e, ct=ct, st=st, dvh=dvh, pb=pb: e.matmul(
                            ps_sc[:, pb + dvh * 512: pb + (dvh + 1) * 512],
                            slab[st % 2][:, ct * 128:(ct + 1) * 128],
                            R1[:, ct * DQ + dvh * 512: ct * DQ + (dvh + 1) * 512],
                            start=(ct == 0), stop=False))
                for dvh in range(2):
                    em.do("tensor", lambda e, dvh=dvh, pb=pb, bvrow=bvrow: e.matmul(
                        ps_sc[:, pb + dvh * 512: pb + (dvh + 1) * 512], ones[:],
                        bvrow[:, dvh * 512:(dvh + 1) * 512], start=False, stop=True),
                        inc=(eP, 1) if dvh == 1 else None)
                vmm_done.append(em.at(eP))
            emit_vevac(em, NST - 1)

        run_phases([("lnTV", ph_AV)], alloc_A)

        def alloc_Bk(ph):
            b = {}
            b["xslab"] = [ph.enter_context(nc.sbuf_tensor(f"kxsl{i}", [P, NCT * 256], BF16)) for i in range(2)]
            return b

        def ph_Bk(em, b):
            xslab = b["xslab"]

            def wk_ap(ct, o0, o1):
                if ct < 28:
                    return R2[:, 15360 + ct * 1024 + o0: 15360 + ct * 1024 + o1]
                return R2[:, (ct - 28) * 1024 + o0: (ct - 28) * 1024 + o1]

            wk_tail = None
            for ct in range(28, NCT):
                wk_tail = em.dma("gpsimd", lambda e, ct=ct: e.dma_start(
                    out=R2[:, (ct - 28) * 1024:(ct - 27) * 1024],
                    in_=wqkv[ct * P:(ct + 1) * P, DQ:2 * DQ]), qG)
            lx = [load_xslab(em, xslab[0][:], 0)]
            em.wt("tensor", wk_tail)
            em.wt("tensor", base_vals["wkpre_mark"])
            mm_done = []
            k_evac = []
            gidx = 0
            for sb in range(16):
                if sb + 1 < 16:
                    if sb >= 1:
                        em.wait("sync", eP, mm_done[8 * (sb - 1) + 7])
                    lx.append(load_xslab(em, xslab[(sb + 1) % 2][:], sb + 1))
                em.wt("tensor", lx[sb])
                for dk in range(NDQ):
                    ps = ps_aa if gidx % 2 == 0 else ps_ab
                    if gidx >= 2:
                        em.wait("tensor", eV, k_evac[gidx - 2])
                    for ct in range(NCT):
                        em.do("tensor", lambda e, ct=ct, dk=dk, sb=sb, ps=ps: e.matmul(
                            ps[:, 0:256], wk_ap(ct, dk * P, (dk + 1) * P),
                            xslab[sb % 2][:, ct * 256:(ct + 1) * 256],
                            start=(ct == 0), stop=(ct == NCT - 1)), inc=(eP, 1) if ct == NCT - 1 else None)
                    mm_done.append(em.at(eP))
                    em.wait("vector", eP, mm_done[gidx])
                    em.do("vector", lambda e, dk=dk, sb=sb, ps=ps: e.tensor_scalar(
                        out=R1[:, dk * S + sb * 256: dk * S + (sb + 1) * 256], in0=ps[:, 0:256],
                        scalar1=bq_sb[:, 8 + dk: 9 + dk], scalar2=None, op0=OP.add), inc=(eV, 1))
                    k_evac.append(em.at(eV))
                    gidx += 1

        run_phases([("projK", ph_Bk)], alloc_Bk)

        def alloc_C(ph):
            b = {}
            b["qT"] = ph.enter_context(nc.sbuf_tensor("qTb", [P, NDQ * T], BF16))
            b["Pb"] = ph.enter_context(nc.sbuf_tensor("Pbb", [P, 2 * S], BF16))
            b["at"] = ph.enter_context(nc.sbuf_tensor("atb", [P, DQ], BF16))
            b["sm"] = ph.enter_context(nc.sbuf_tensor("smb", [P, 32], F32))
            return b

        def ph_Bq(em, b):
            qT = b["qT"]
            xqs = [R2[:, 32768:40960], R1[:, 32768:40960]]
            wq_mark = load_w(em, 0, DQ, R2, 0)[-1]

            def load_xq(em, sb):
                return em.dma("sync", lambda e, sb=sb: e.dma_start(
                    out=xqs[sb % 2].rearrange("p (a s) -> p a s", a=NCT),
                    in_=xnt_h[:, :, sb * 256:(sb + 1) * 256].rearrange("a p s -> p a s")), qL)

            mm_done = []
            q_evac = []
            lx = [load_xq(em, 0)]
            gidx = 0
            for sb in range(4):
                if sb + 1 < 4:
                    if sb >= 1:
                        em.wait("sync", eP, mm_done[8 * (sb - 1) + 7])
                    lx.append(load_xq(em, sb + 1))
                em.wt("tensor", lx[sb])
                if sb == 0:
                    em.wt("tensor", wq_mark)
                for dk in range(NDQ):
                    ps = ps_aa if gidx % 2 == 0 else ps_ab
                    if gidx >= 2:
                        em.wait("tensor", eV, q_evac[gidx - 2])
                    for ct in range(NCT):
                        em.do("tensor", lambda e, ct=ct, dk=dk, ps=ps, sb=sb: e.matmul(
                            ps[:, 0:256], R2[:, ct * DQ + dk * P: ct * DQ + (dk + 1) * P],
                            xqs[sb % 2][:, ct * 256:(ct + 1) * 256],
                            start=(ct == 0), stop=(ct == NCT - 1)), inc=(eP, 1) if ct == NCT - 1 else None)
                    mm_done.append(em.at(eP))
                    em.wait("vector", eP, mm_done[gidx])
                    em.do("vector", lambda e, dk=dk, sb=sb, ps=ps: e.tensor_scalar(
                        out=qT[:, dk * T + sb * 256: dk * T + (sb + 1) * 256], in0=ps[:, 0:256],
                        scalar1=bq_sb[:, dk: dk + 1], scalar2=None, op0=OP.add), inc=(eV, 1))
                    q_evac.append(em.at(eV))
                    gidx += 1

        def ph_C(em, b):
            qT, Pb, at, sm = b["qT"], b["Pb"], b["at"], b["sm"]
            attnT = R1[:, 32768:40960]
            v_mark = em.dma("sync", lambda e: e.dma_start(
                out=R2[:, 0:NST * DQ].rearrange("p (a d) -> p a d", a=NST),
                in_=v_h.rearrange("a p d -> p a d")), qL)
            exp_done = {}
            pt_evac = []
            amm_done = {}
            at_evac = {}
            att_done = {}
            attT_evac = {}
            pfin = {}

            def smv(qt, i):
                blk = 16 * (qt % 2)
                return sm[:, blk + i: blk + i + 1]

            def pbuf(qt):
                return Pb[:, (qt % 2) * S:(qt % 2 + 1) * S]

            def emit_scores(em, qt, half):
                if (qt, half) != (0, 0):
                    prev = (qt, 0) if half == 1 else (qt - 1, 1)
                    em.wait("tensor", eA, exp_done[prev])
                for sbl in range(4):
                    sb = half * 4 + sbl
                    for dk in range(NDQ):
                        em.do("tensor", lambda e, sbl=sbl, sb=sb, dk=dk, qt=qt: e.matmul(
                            ps_sc[:, sbl * 512:(sbl + 1) * 512],
                            qT[:, dk * T + qt * P: dk * T + (qt + 1) * P],
                            R1[:, dk * S + sb * 512: dk * S + (sb + 1) * 512],
                            start=(dk == 0), stop=(dk == NDQ - 1)),
                            inc=(eP, 1) if (dk == NDQ - 1 and sbl == 3) else None)
                return em.at(eP)

            def emit_softmax_h(em, qt, half, sc_done):
                mdst = smv(qt, half)
                nm = smv(qt, 8 + half)
                sdst = smv(qt, 4 + half)
                em.wait("vector", eP, sc_done)
                em.do("vector", lambda e, mdst=mdst: e.reduce_max(out=mdst, in_=ps_sc[:], axis=AX), inc=(eV, 1))
                em.wait("vector", eV, em.at(eV))
                em.do("vector", lambda e, mdst=mdst, nm=nm: e.tensor_scalar(
                    out=nm, in0=mdst, scalar1=-SCL, scalar2=None, op0=OP.mult), inc=(eV, 1))
                em.wait("scalar", eV, em.at(eV))
                if qt >= 2 and half == 0:
                    em.wait("scalar", eP, amm_done[qt - 2])
                em.do("scalar", lambda e, qt=qt, half=half, nm=nm: e.activation(
                    out=pbuf(qt)[:, half * 2048:(half + 1) * 2048], in_=ps_sc[:],
                    func=AF.Exp, bias=nm, scale=SCL), inc=(eA, 1))
                exp_done[(qt, half)] = em.at(eA)
                em.wait("vector", eA, exp_done[(qt, half)])
                em.do("vector", lambda e, qt=qt, half=half, sdst=sdst: e.reduce_sum(
                    out=sdst, in_=pbuf(qt)[:, half * 2048:(half + 1) * 2048], axis=AX), inc=(eV, 1))
                em.wait("vector", eV, em.at(eV))

            def emit_merge(em, qt):
                m0, m1, mx, l_ = smv(qt, 0), smv(qt, 1), smv(qt, 2), smv(qt, 3)
                t0_, t1_, c0, c1 = smv(qt, 4), smv(qt, 5), smv(qt, 6), smv(qt, 7)
                linv = smv(qt, 10)
                em.do("vector", lambda e, m0=m0, m1=m1, mx=mx: e.tensor_tensor(
                    out=mx, in0=m0, in1=m1, op=OP.max), inc=(eV, 1))
                em.wait("vector", eV, em.at(eV))
                em.do("vector", lambda e, m0=m0, mx=mx, c0=c0: e.tensor_tensor(
                    out=c0, in0=m0, in1=mx, op=OP.subtract))
                em.do("vector", lambda e, m1=m1, mx=mx, c1=c1: e.tensor_tensor(
                    out=c1, in0=m1, in1=mx, op=OP.subtract), inc=(eV, 1))
                em.wait("scalar", eV, em.at(eV))
                em.do("scalar", lambda e, c0=c0: e.activation(out=c0, in_=c0, func=AF.Exp, scale=SCL))
                em.do("scalar", lambda e, c1=c1: e.activation(out=c1, in_=c1, func=AF.Exp, scale=SCL), inc=(eA, 1))
                em.wait("vector", eA, em.at(eA))
                em.do("vector", lambda e, t0_=t0_, c0=c0: e.tensor_scalar(
                    out=t0_, in0=t0_, scalar1=c0, scalar2=None, op0=OP.mult))
                em.do("vector", lambda e, t1_=t1_, c1=c1: e.tensor_scalar(
                    out=t1_, in0=t1_, scalar1=c1, scalar2=None, op0=OP.mult), inc=(eV, 1))
                em.wait("vector", eV, em.at(eV))
                em.do("vector", lambda e, l_=l_, t0_=t0_, t1_=t1_: e.tensor_tensor(
                    out=l_, in0=t0_, in1=t1_, op=OP.add), inc=(eV, 1))
                em.wait("vector", eV, em.at(eV))
                em.do("vector", lambda e, linv=linv, l_=l_: e.reciprocal(out=linv, in_=l_), inc=(eV, 1))
                em.wait("vector", eV, em.at(eV))
                em.do("vector", lambda e, qt=qt, c0=c0: e.tensor_scalar(
                    out=pbuf(qt)[:, 0:2048], in0=pbuf(qt)[:, 0:2048], scalar1=c0, scalar2=None, op0=OP.mult))
                em.do("vector", lambda e, qt=qt, c1=c1: e.tensor_scalar(
                    out=pbuf(qt)[:, 2048:4096], in0=pbuf(qt)[:, 2048:4096], scalar1=c1, scalar2=None, op0=OP.mult), inc=(eV, 1))
                pfin[qt] = em.at(eV)

            def emit_PT(em, qt):
                em.wait("tensor", eV, pfin[qt])
                if qt >= 1:
                    em.wait("tensor", eA, attT_evac[qt - 1])
                for k in range(4):
                    gk = qt * 4 + k
                    bank = ps_ta if k % 2 == 0 else ps_tb
                    if gk >= 2:
                        em.wait("tensor", eA, pt_evac[gk - 2])
                    for j in range(8):
                        sc = k * 8 + j
                        em.do("tensor", lambda e, qt=qt, sc=sc, j=j, bank=bank: e.transpose(
                            bank[:, j * 128:(j + 1) * 128], pbuf(qt)[:, sc * 128:(sc + 1) * 128], ident[:]),
                            inc=(eP, 1) if j == 7 else None)
                    tb_done = em.at(eP)
                    em.wait("scalar", eP, tb_done)
                    em.do("scalar", lambda e, qt=qt, bank=bank, k=k: e.copy(
                        out=pbuf(qt)[:, k * 1024:(k + 1) * 1024], in_=bank[:]), inc=(eA, 1))
                    pt_evac.append(em.at(eA))

            def emit_PV(em, qt):
                em.wait("tensor", eA, pt_evac[qt * 4 + 3])
                em.wt("tensor", v_mark)
                if qt >= 1:
                    em.wait("tensor", eV, at_evac[qt - 1])
                for sc in range(NST):
                    em.do("tensor", lambda e, qt=qt, sc=sc: e.matmul(
                        ps_aa[:], pbuf(qt)[:, sc * P:(sc + 1) * P],
                        R2[:, sc * DQ: sc * DQ + 512], start=(sc == 0), stop=(sc == NST - 1)))
                    em.do("tensor", lambda e, qt=qt, sc=sc: e.matmul(
                        ps_ab[:], pbuf(qt)[:, sc * P:(sc + 1) * P],
                        R2[:, sc * DQ + 512:(sc + 1) * DQ], start=(sc == 0), stop=(sc == NST - 1)),
                        inc=(eP, 1) if sc == NST - 1 else None)
                amm_done[qt] = em.at(eP)
                linv = smv(qt, 10)
                em.wait("vector", eP, amm_done[qt])
                if qt >= 1:
                    em.wait("vector", eP, att_done[qt - 1])
                em.do("vector", lambda e, linv=linv: e.tensor_scalar(
                    out=at[:, 0:512], in0=ps_aa[:], scalar1=linv, scalar2=None, op0=OP.mult))
                em.do("vector", lambda e, linv=linv: e.tensor_scalar(
                    out=at[:, 512:1024], in0=ps_ab[:], scalar1=linv, scalar2=None, op0=OP.mult), inc=(eV, 1))
                at_evac[qt] = em.at(eV)
                em.wait("tensor", eV, at_evac[qt])
                for j in range(NDQ):
                    em.do("tensor", lambda e, j=j: e.transpose(
                        ps_ta[:, j * 128:(j + 1) * 128], at[:, j * 128:(j + 1) * 128], ident[:]),
                        inc=(eP, 1) if j == NDQ - 1 else None)
                att_done[qt] = em.at(eP)
                em.wait("scalar", eP, att_done[qt])
                em.do("scalar", lambda e, qt=qt, attnT=attnT: e.copy(
                    out=attnT.rearrange("p (a t) -> p a t", a=NDQ)[:, :, qt * P:(qt + 1) * P],
                    in_=ps_ta[:].rearrange("p (a t) -> p a t", a=NDQ)), inc=(eA, 1))
                attT_evac[qt] = em.at(eA)

            for qt in range(NTT):
                scd0 = emit_scores(em, qt, 0)
                emit_softmax_h(em, qt, 0, scd0)
                if qt >= 1:
                    emit_PT(em, qt - 1)
                scd1 = emit_scores(em, qt, 1)
                emit_softmax_h(em, qt, 1, scd1)
                emit_merge(em, qt)
                if qt >= 1:
                    emit_PV(em, qt - 1)
            emit_PT(em, NTT - 1)
            emit_PV(em, NTT - 1)
            base_vals["amm_last"] = amm_done[NTT - 1]
            base_vals["exp_last"] = exp_done[(NTT - 1, 1)]

        run_phases([("projQ", ph_Bq), ("attnC", ph_C)], alloc_C)

        def ph_D(em, b):
            attnT = R1[:, 32768:40960]
            sat = [R2[:, 40960:41984], R2[:, 41984:43008]]
            bo_sb = R2[0:1, 36864:40960]
            em.wait("gpsimd", eP, base_vals["amm_last"])
            bo_mark = em.dma("gpsimd", lambda e, bo_sb=bo_sb: e.dma_start(out=bo_sb, in_=bo_r), qG)
            wo_marks = []
            for dvc in range(NDQ):
                wo_marks.append(em.dma("gpsimd", lambda e, dvc=dvc: e.dma_start(
                    out=R2[:, dvc * C:(dvc + 1) * C],
                    in_=wo[dvc * P:(dvc + 1) * P, :]), qG))
            for ct in range(NCT):
                base_vals["xnt_mark"] = em.dma("sync", lambda e, ct=ct: e.dma_start(
                    out=R1[:, ct * T:(ct + 1) * T], in_=xnt_h[ct, :, 0:T]), qL)
            em.wait("tensor", eA, base_vals["exp_last"])
            ds = []
            evac_done = []
            for tt in range(NTT):
                for qtr in range(4):
                    g = tt * 4 + qtr
                    half = qtr % 2
                    if g >= 2:
                        em.wait("tensor", eA, evac_done[g - 2])
                    for dvc in range(NDQ):
                        if g == 0:
                            em.wt("tensor", wo_marks[dvc])
                        for cbl in range(2):
                            cb = qtr * 2 + cbl
                            em.do("tensor", lambda e, half=half, cbl=cbl, cb=cb, dvc=dvc, tt=tt: e.matmul(
                                ps_sc[:, (half * 2 + cbl) * 512:(half * 2 + cbl + 1) * 512],
                                attnT[:, dvc * T + tt * P: dvc * T + (tt + 1) * P],
                                R2[:, dvc * C + cb * 512: dvc * C + (cb + 1) * 512],
                                start=(dvc == 0), stop=False))
                    if g == 0:
                        em.wt("tensor", bo_mark)
                    for cbl in range(2):
                        cb = qtr * 2 + cbl
                        em.do("tensor", lambda e, half=half, cbl=cbl, cb=cb, bo_sb=bo_sb: e.matmul(
                            ps_sc[:, (half * 2 + cbl) * 512:(half * 2 + cbl + 1) * 512], ones[:],
                            bo_sb[:, cb * 512:(cb + 1) * 512], start=False, stop=True),
                            inc=(eP, 1) if cbl == 1 else None)
                    mm_done = em.at(eP)
                    em.wait("scalar", eP, mm_done)
                    if g >= 2:
                        em.wt("scalar", ds[g - 2])
                    em.do("scalar", lambda e, half=half: e.copy(
                        out=sat[half], in_=ps_sc[:, half * 1024:(half + 1) * 1024]), inc=(eA, 1))
                    evac_done.append(em.at(eA))
                    em.wait("sync", eA, evac_done[g])
                    ds.append(em.dma("sync", lambda e, tt=tt, qtr=qtr, half=half: e.dma_start(
                        out=sat_h[tt * P:(tt + 1) * P, qtr * 1024:(qtr + 1) * 1024],
                        in_=sat[half]), dS[half]))

        run_phases([("projO", ph_D)])

        def alloc_E(ph):
            b = {}
            b["szg"] = [ph.enter_context(nc.sbuf_tensor(f"szg{i}", [P, 64], F32)) for i in range(2)]
            b["szu"] = [ph.enter_context(nc.sbuf_tensor(f"szu{i}", [P, 64], F32)) for i in range(2)]
            b["h"] = [ph.enter_context(nc.sbuf_tensor(f"hh{i}", [P, 512], BF16)) for i in range(2)]
            b["sg"] = ph.enter_context(nc.sbuf_tensor("sgg", [P, 1024], BF16))
            b["gu2"] = ph.enter_context(nc.sbuf_tensor("gu2", [P, 8192], BF16))
            return b

        def ph_E(em, b):
            szg, szu, hbuf, sgb = b["szg"], b["szu"], b["h"], b["sg"]
            gu2 = b["gu2"]
            xnt_mark = base_vals["xnt_mark"]
            QG0, QU0, WG0, WU0 = 0, 8192, 24576, 32768
            GDs = [(R2, 16384), (gu2, 0)]
            UDs = [(R2, 20480), (gu2, 4096)]
            deq_done = []
            tr_done = []
            tr_evac = []
            mm_done = []
            silu_done = []
            h_done = []
            hs = []
            cast_marks = []
            sz_marks = []

            def emit_dmas(em, it):
                if it >= 2:
                    em.wait("sync", eV, deq_done[2 * (it - 2) + 1])
                em.dma("sync", lambda e, it=it: e.dma_start(
                    out=szg[it % 2][:, 0:32], in_=stg[it * P:(it + 1) * P, :]), eZ[it % 2])
                em.dma("sync", lambda e, it=it: e.dma_start(
                    out=szg[it % 2][:, 32:64], in_=ztg[it * P:(it + 1) * P, :]), eZ[it % 2])
                em.dma("sync", lambda e, it=it: e.dma_start(
                    out=szu[it % 2][:, 0:32], in_=stu[it * P:(it + 1) * P, :]), eZ[it % 2])
                sz_marks.append(em.dma("sync", lambda e, it=it: e.dma_start(
                    out=szu[it % 2][:, 32:64], in_=ztu[it * P:(it + 1) * P, :]), eZ[it % 2]))
                if it >= 2:
                    em.wait("gpsimd", eV, deq_done[2 * (it - 2) + 1])
                em.dma("gpsimd", lambda e, it=it: e.dma_start(
                    out=R2[:, QG0 + (it % 2) * C: QG0 + (it % 2 + 1) * C],
                    in_=qtg[it * P:(it + 1) * P, :]), eC[it % 2])
                cast_marks.append(em.dma("gpsimd", lambda e, it=it: e.dma_start(
                    out=R2[:, QU0 + (it % 2) * C: QU0 + (it % 2 + 1) * C],
                    in_=qtu[it * P:(it + 1) * P, :]), eC[it % 2]))

            def emit_h(em, it, r):
                idx = 2 * it + r
                em.wait("vector", eA, silu_done[idx])
                if it >= 1:
                    em.wt("vector", hs[idx - 2])
                em.do("vector", lambda e, it=it, r=r: e.tensor_tensor(
                    out=hbuf[r][:], in0=sgb[:, r * 512:(r + 1) * 512],
                    in1=ps_sc[:, (2 * r + 1) * 512:(2 * r + 2) * 512], op=OP.mult), inc=(eV, 1))
                h_done.append(em.at(eV))
                em.wait("sync", eV, h_done[idx])
                mk = None
                for j in range(4):
                    mk = em.dma("sync", lambda e, it=it, r=r, j=j: e.dma_start(
                        out=h_h[4 * r + j, :, it * P:(it + 1) * P],
                        in_=hbuf[r][:, j * 128:(j + 1) * 128]), eH[r])
                hs.append(mk)

            def emit_deq(em, it):
                em.wt("vector", cast_marks[it])
                em.wt("vector", sz_marks[it])
                if it >= 2:
                    em.wait("vector", eP, tr_done[8 * (it - 2) + 7])
                em.do("vector", lambda e, it=it: e.scalar_tensor_tensor(
                    out=szg[it % 2][:, 32:64], in0=szg[it % 2][:, 0:32], scalar=-8.0,
                    in1=szg[it % 2][:, 32:64], op0=OP.mult, op1=OP.add))
                em.do("vector", lambda e, it=it: e.scalar_tensor_tensor(
                    out=szu[it % 2][:, 32:64], in0=szu[it % 2][:, 0:32], scalar=-8.0,
                    in1=szu[it % 2][:, 32:64], op0=OP.mult, op1=OP.add), inc=(eV, 1))
                em.wait("vector", eV, em.at(eV))
                for src0, dslot, szb in ((QG0, GDs[it % 2], szg), (QU0, UDs[it % 2], szu)):
                    dmat, dst0 = dslot
                    for g in range(NGU):
                        em.do("vector", lambda e, it=it, g=g, src0=src0, dmat=dmat, dst0=dst0, szb=szb: e.tensor_scalar(
                            out=dmat[:, dst0 + g * P: dst0 + (g + 1) * P],
                            in0=R2[:, src0 + (it % 2) * C + g * P: src0 + (it % 2) * C + (g + 1) * P],
                            scalar1=szb[it % 2][:, g:g + 1],
                            scalar2=szb[it % 2][:, 32 + g:33 + g],
                            op0=OP.mult, op1=OP.add), inc=(eV, 1) if g == NGU - 1 else None)
                    deq_done.append(em.at(eV))

            emit_dmas(em, 0)
            emit_deq(em, 0)
            for it in range(NIT):
                if it + 1 < NIT:
                    emit_dmas(em, it + 1)
                if it >= 1:
                    emit_h(em, it - 1, 1)
                if it + 1 < NIT:
                    emit_deq(em, it + 1)
                for k in range(8):
                    pp, which = k // 2, k % 2
                    dmat, mat_off = (GDs if which == 0 else UDs)[it % 2]
                    src_deq = deq_done[2 * it + which]
                    em.wait("tensor", eV, src_deq)
                    gk = it * 8 + k
                    bank = ps_ta if k % 2 == 0 else ps_tb
                    if gk >= 2:
                        em.wait("tensor", eA, tr_evac[gk - 2])
                    for j in range(8):
                        ct = pp * 8 + j
                        em.do("tensor", lambda e, ct=ct, j=j, bank=bank, dmat=dmat, mat_off=mat_off: e.transpose(
                            bank[:, j * 128:(j + 1) * 128],
                            dmat[:, mat_off + ct * P: mat_off + (ct + 1) * P], ident[:]),
                            inc=(eP, 1) if j == 7 else None)
                    tr_done.append(em.at(eP))
                    em.wait("scalar", eP, tr_done[gk])
                    wdst = WG0 if which == 0 else WU0
                    em.do("scalar", lambda e, bank=bank, pp=pp, it=it, wdst=wdst: e.copy(
                        out=R2[:, wdst + (it % 2) * C + pp * 1024: wdst + (it % 2) * C + (pp + 1) * 1024],
                        in_=bank[:]), inc=(eA, 1))
                    tr_evac.append(em.at(eA))
                em.wt("tensor", xnt_mark)
                for r in range(2):
                    if it >= 1:
                        em.wait("tensor", eA, silu_done[2 * (it - 1) + r])
                        em.wait("tensor", eV, h_done[2 * (it - 1) + r])
                    for ct in range(NCT):
                        if r == 0 and ct % 8 == 0:
                            em.wait("tensor", eA, tr_evac[it * 8 + 2 * (ct // 8) + 1])
                        for mi, woff in ((0, WG0), (1, WU0)):
                            em.do("tensor", lambda e, ct=ct, mi=mi, woff=woff, r=r, it=it: e.matmul(
                                ps_sc[:, (2 * r + mi) * 512:(2 * r + mi + 1) * 512],
                                R2[:, woff + (it % 2) * C + ct * P: woff + (it % 2) * C + (ct + 1) * P],
                                R1[:, ct * T + r * 512: ct * T + (r + 1) * 512],
                                start=(ct == 0), stop=(ct == NCT - 1)),
                                inc=(eP, 1) if (ct == NCT - 1 and mi == 1) else None)
                    mm_done.append(em.at(eP))
                    em.wait("scalar", eP, mm_done[2 * it + r])
                    if it >= 1:
                        em.wait("scalar", eV, h_done[2 * (it - 1) + r])
                    em.do("scalar", lambda e, r=r: e.activation(
                        out=sgb[:, r * 512:(r + 1) * 512],
                        in_=ps_sc[:, 2 * r * 512:(2 * r + 1) * 512], func=AF.Silu), inc=(eA, 1))
                    silu_done.append(em.at(eA))
                    if r == 0:
                        emit_h(em, it, 0)
            emit_h(em, NIT - 1, 1)

        run_phases([("ffnGU", ph_E)], alloc_E)

        def alloc_F(ph):
            b = {}
            b["qd"] = ph.enter_context(nc.sbuf_tensor("qd0", [P, II], BF16))
            b["szd"] = [ph.enter_context(nc.sbuf_tensor(f"szd{i}", [P, 2 * NGD], F32)) for i in range(3)]
            b["g"] = [ph.enter_context(nc.sbuf_tensor(f"gg{i}", [P, 512], BF16)) for i in range(2)]
            b["sgd"] = ph.enter_context(nc.sbuf_tensor("sgdd", [P, 512], F32))
            b["satl"] = [ph.enter_context(nc.sbuf_tensor(f"satl{i}", [P, 512], BF16)) for i in range(2)]
            b["xres"] = [ph.enter_context(nc.sbuf_tensor(f"xres{i}", [P, 512], F32)) for i in range(2)]
            return b

        def ph_F(em, b):
            szd, gbuf, sgd, satl, xres = (
                b["szd"], b["g"], b["sgd"], b["satl"], b["xres"])
            hbufs = [R2[:, 0:II], R2[:, II:2 * II]]
            qdbs = [(b["qd"], 0), (R2, 2 * II), (R2, 3 * II)]

            def qd_ap(k, c0, c1):
                t, off = qdbs[k % 3]
                return t[:, off + c0: off + c1]

            NB = (NGD + 7) // 8
            PIECES = [(0, 3072), (3072, 6144), (6144, 9216), (9216, II)]
            deq_banks = {}
            ch_copy = {}
            tr_done = []
            tr_evac = []
            dn_done = []
            silu_done = []
            comb_done = []
            h_marks = []
            c_marks = []
            o_marks = []
            qd_marks = {}

            def emit_chain_dma(em, k):
                d0 = k * P
                if k >= 3:
                    em.wait("sync", eV, deq_banks[k - 3][-1])
                em.dma("sync", lambda e, k=k, d0=d0: e.dma_start(
                    out=szd[k % 3][:, 0:NGD], in_=std[d0:d0 + P, :]), fZ[k % 3])
                em.dma("sync", lambda e, k=k, d0=d0: e.dma_start(
                    out=szd[k % 3][:, NGD:2 * NGD], in_=ztd[d0:d0 + P, :]), fZ[k % 3])
                marks = []
                for pi, (c0, c1) in enumerate(PIECES):
                    if k >= 3:
                        lb = min(((c1 + 1023) // 1024) - 1, NB - 1)
                        em.wait("gpsimd", eP, tr_done[NB * (k - 3) + lb])
                    marks.append(em.dma("gpsimd", lambda e, k=k, d0=d0, c0=c0, c1=c1: e.dma_start(
                        out=qd_ap(k, c0, c1), in_=qtd[d0:d0 + P, c0:c1]), fQ[k % 3][pi]))
                qd_marks[k] = marks

            def emit_chain_deq(em, k):
                sd_ = szd[k % 3][:, 0:NGD]
                z8d_ = szd[k % 3][:, NGD:2 * NGD]
                em.wait("vector", fZ[k % 3], (k // 3 + 1) * 32)
                em.do("vector", lambda e, sd_=sd_, z8d_=z8d_: e.scalar_tensor_tensor(
                    out=z8d_, in0=sd_, scalar=-8.0, in1=z8d_, op0=OP.mult, op1=OP.add), inc=(eV, 1))
                em.wait("vector", eV, em.at(eV))
                banks = []
                for kb in range(NB):
                    j0, j1 = kb * 8, min(kb * 8 + 8, NGD)
                    pi = next(i for i, (c0, c1) in enumerate(PIECES) if j1 * P <= c1)
                    em.wt("vector", qd_marks[k][pi])
                    for g in range(j0, j1):
                        em.do("vector", lambda e, k=k, g=g, sd_=sd_, z8d_=z8d_: e.tensor_scalar(
                            out=qd_ap(k, g * P, (g + 1) * P), in0=qd_ap(k, g * P, (g + 1) * P),
                            scalar1=sd_[:, g:g + 1], scalar2=z8d_[:, g:g + 1],
                            op0=OP.mult, op1=OP.add), inc=(eV, 1) if g == j1 - 1 else None)
                    banks.append(em.at(eV))
                deq_banks[k] = banks

            def emit_chain_trans(em, k):
                dt = k % 4
                for kb in range(NB):
                    j0 = kb * 8
                    j1 = min(j0 + 8, NGD)
                    em.wait("tensor", eV, deq_banks[k][kb])
                    bank = ps_ta if len(tr_done) % 2 == 0 else ps_tb
                    if len(tr_evac) >= 2:
                        em.wait("tensor", eA, tr_evac[-2])
                    for j in range(j0, j1):
                        em.do("tensor", lambda e, k=k, j=j, j0=j0, bank=bank: e.transpose(
                            bank[:, (j - j0) * 128:(j - j0 + 1) * 128],
                            qd_ap(k, j * P, (j + 1) * P), ident[:]),
                            inc=(eP, 1) if j == j1 - 1 else None)
                    tr_done.append(em.at(eP))
                    em.wait("scalar", eP, tr_done[-1])
                    w = (j1 - j0) * 128
                    em.do("scalar", lambda e, bank=bank, j0=j0, j1=j1, dt=dt, w=w: e.copy(
                        out=R1[:].rearrange("p (a t) -> p a t", a=NGD)[:, j0:j1, dt * P:(dt + 1) * P],
                        in_=bank[:, 0:w].rearrange("p (a t) -> p a t", a=j1 - j0)), inc=(eA, 1))
                    tr_evac.append(em.at(eA))
                ch_copy[k] = em.at(eA)

            HHALF = 43 * P

            def emit_hload(em, gti):
                if gti >= 2:
                    em.wait("sync", eP, dn_done[gti - 2])
                m0 = em.dma("sync", lambda e, gti=gti: e.dma_start(
                    out=hbufs[gti % 2][:, 0:HHALF], in_=h_h[gti % 8][:, 0:HHALF]), fH[gti % 2])
                m1 = em.dma("sync", lambda e, gti=gti: e.dma_start(
                    out=hbufs[gti % 2][:, HHALF:II], in_=h_h[gti % 8][:, HHALF:II]), fH2[gti % 2])
                h_marks.append((m0, m1))

            for k in range(3):
                emit_chain_dma(em, k)
            for k in range(3):
                emit_chain_deq(em, k)
            emit_chain_trans(em, 0)
            emit_chain_dma(em, 3)
            emit_chain_deq(em, 3)
            emit_chain_trans(em, 1)
            emit_chain_trans(em, 2)
            emit_chain_trans(em, 3)
            emit_hload(em, 0)
            emit_hload(em, 1)

            for db in range(8):
                for ti in range(8):
                    gti = db * 8 + ti
                    ps = ps_aa if ti % 2 == 0 else ps_ab
                    if ti == 0:
                        em.wait("tensor", eA, ch_copy[4 * db + 3])
                    em.wt("tensor", h_marks[gti][0])
                    if gti >= 2:
                        em.wait("tensor", eA, silu_done[gti - 2])
                    for a in range(NIT):
                        if a == 43:
                            em.wt("tensor", h_marks[gti][1])
                        em.do("tensor", lambda e, a=a, ti=ti, ps=ps: e.matmul(
                            ps[:], hbufs[gti % 2][:, a * P:(a + 1) * P],
                            R1[:, a * 512:(a + 1) * 512],
                            start=(a == 0), stop=(a == NIT - 1)), inc=(eP, 1) if a == NIT - 1 else None)
                    dn_done.append(em.at(eP))
                    if gti + 2 < 64:
                        emit_hload(em, gti + 2)
                    if len(comb_done) >= 2:
                        em.wait("sync", eV, comb_done[-2])
                    if len(o_marks) >= 2:
                        em.wt("sync", o_marks[-2])
                    em.dma("sync", lambda e, ti=ti, db=db: e.dma_start(
                        out=satl[ti % 2][:], in_=sat_h[ti * P:(ti + 1) * P, db * 512:(db + 1) * 512]), fC[ti % 2])
                    c_marks.append(em.dma("sync", lambda e, ti=ti, db=db: e.dma_start(
                        out=xres[ti % 2][:], in_=xb[ti * P:(ti + 1) * P, db * 512:(db + 1) * 512]), fC[ti % 2]))
                    em.wait("scalar", eP, dn_done[-1])
                    if gti >= 2:
                        em.wait("scalar", eV, comb_done[gti - 2])
                    em.do("scalar", lambda e, ti=ti, ps=ps: e.activation(
                        out=gbuf[ti % 2][:], in_=ps[:], func=AF.Silu), inc=(eA, 1))
                    silu_done.append(em.at(eA))
                    em.wait("vector", eA, silu_done[-1])
                    em.wt("vector", c_marks[-1])
                    if len(o_marks) >= 2:
                        em.wt("vector", o_marks[-2])
                    em.do("vector", lambda e, ti=ti: e.tensor_tensor(
                        out=gbuf[ti % 2][:], in0=satl[ti % 2][:], in1=gbuf[ti % 2][:], op=OP.mult), inc=(eV, 1))
                    em.wait("vector", eV, em.at(eV))
                    em.do("vector", lambda e, ti=ti: e.tensor_copy(out=sgd[:], in_=gbuf[ti % 2][:]), inc=(eV, 1))
                    em.wait("vector", eV, em.at(eV))
                    em.do("vector", lambda e, ti=ti: e.tensor_tensor(
                        out=xres[ti % 2][:], in0=xres[ti % 2][:], in1=sgd[:], op=OP.add), inc=(eV, 1))
                    comb_done.append(em.at(eV))
                    em.wait("sync", eV, comb_done[-1])
                    o_marks.append(em.dma("sync", lambda e, ti=ti, db=db: e.dma_start(
                        out=out_d[ti * P:(ti + 1) * P, db * 512:(db + 1) * 512],
                        in_=xres[ti % 2][:]), fO[ti % 2]))
                    if db < 7 and ti < 3:
                        kk = 4 * (db + 1) + ti
                        emit_chain_dma(em, kk)
                        emit_chain_deq(em, kk)
                if db < 7:
                    emit_chain_trans(em, 4 * (db + 1))
                    emit_chain_dma(em, 4 * (db + 1) + 3)
                    emit_chain_trans(em, 4 * (db + 1) + 1)
                    emit_chain_trans(em, 4 * (db + 1) + 2)
                    emit_chain_deq(em, 4 * (db + 1) + 3)
                    emit_chain_trans(em, 4 * (db + 1) + 3)

        run_phases([("ffnDN", ph_F)], alloc_F)

    return nc


_NC_CACHE = {}


def kernel(L0_attn_x, L0_attn_Wqkv, L0_attn_bqkv, L0_attn_Wo, L0_attn_bo,
           L0_ffn_gate_q, L0_ffn_gate_s, L0_ffn_gate_z,
           L0_ffn_up_q, L0_ffn_up_s, L0_ffn_up_z,
           L0_ffn_down_q, L0_ffn_down_s, L0_ffn_down_z):
    x = np.asarray(L0_attn_x, dtype=np.float32)
    wqkv = np.ascontiguousarray(np.asarray(L0_attn_Wqkv, dtype=np.float32))
    bqkv = np.asarray(L0_attn_bqkv, dtype=np.float32)
    wo_np = np.ascontiguousarray(np.asarray(L0_attn_Wo, dtype=np.float32))
    bo = np.asarray(L0_attn_bo, dtype=np.float32)
    qtg = np.ascontiguousarray(np.asarray(L0_ffn_gate_q, dtype=np.int32).T)
    stg = np.ascontiguousarray(np.asarray(L0_ffn_gate_s, dtype=np.float32).T)
    ztg = np.ascontiguousarray(np.asarray(L0_ffn_gate_z, dtype=np.float32).T)
    qtu = np.ascontiguousarray(np.asarray(L0_ffn_up_q, dtype=np.int32).T)
    stu = np.ascontiguousarray(np.asarray(L0_ffn_up_s, dtype=np.float32).T)
    ztu = np.ascontiguousarray(np.asarray(L0_ffn_up_z, dtype=np.float32).T)
    qtd = np.ascontiguousarray(np.asarray(L0_ffn_down_q, dtype=np.int32).T)
    std = np.ascontiguousarray(np.asarray(L0_ffn_down_s, dtype=np.float32).T)
    ztd = np.ascontiguousarray(np.asarray(L0_ffn_down_z, dtype=np.float32).T)

    bq_c = np.ascontiguousarray(bqkv.reshape(24, P).T)
    bq_r = np.ascontiguousarray(bqkv.reshape(1, 3 * DQ))
    bo_r = np.ascontiguousarray(bo.reshape(1, C))
    ident = np.eye(P, dtype=ml_dtypes.bfloat16)
    ones = np.ones((1, P), dtype=ml_dtypes.bfloat16)

    if "nc" not in _NC_CACHE:
        _NC_CACHE["nc"] = build_program()
    nc = _NC_CACHE["nc"]

    in_maps = []
    for c in range(8):
        bi = c // 4
        t0 = (c % 4) * T
        xb_c = np.ascontiguousarray(np.roll(x[bi], -t0, axis=0))
        in_maps.append(dict(
            xb=xb_c, wqkv=wqkv, bq_c=bq_c, bq_r=bq_r, wo=wo_np, bo_r=bo_r,
            qtg=qtg, stg=stg, ztg=ztg, qtu=qtu, stu=stu, ztu=ztu,
            qtd=qtd, std=std, ztd=ztd, ident=ident, ones=ones,
        ))
    res = run_bass_kernel_spmd(nc, in_maps, core_ids=list(range(8)))
    out = np.empty((2, S, C), dtype=np.float32)
    for c in range(8):
        bi = c // 4
        t0 = (c % 4) * T
        out[bi, t0:t0 + T, :] = res.results[c]["out"]
    return out


if __name__ == "__main__":
    nc = build_program()
    n = sum(1 for _ in nc.all_instructions())
    print("built, instructions:", n)


# version 12
# speedup vs baseline: 1.4008x; 1.0137x over previous
import numpy as np
import ml_dtypes
from contextlib import ExitStack

import concourse.bass as bass
import concourse.mybir as mybir
from concourse.bass_utils import run_bass_kernel_spmd

F32 = mybir.dt.float32
BF16 = mybir.dt.bfloat16
I32 = mybir.dt.int32
AX = mybir.AxisListType.X
OP = mybir.AluOpType
AF = mybir.ActivationFunctionType

P = 128
C = 4096
NCT = C // P
DQ = 1024
NDQ = DQ // P
S = 4096
NST = S // P
T = 1024
NTT = T // P
II = 11008
NIT = II // P
NGU = 32
NGD = 86
EPS = 1e-5
SCL = 1.0 / 32.0

ENGS = ("sync", "tensor", "vector", "scalar", "gpsimd")


class Em:

    def __init__(self, who, e, cnt):
        self.who = who
        self.e = e
        self.cnt = cnt

    def do(self, eng, fn, inc=None):
        if inc is not None:
            sem, n = inc
            if self.who == eng:
                fn(self.e).then_inc(sem, n)
            self.cnt[id(sem)] = self.cnt.get(id(sem), 0) + n
        elif self.who == eng:
            fn(self.e)

    def wait(self, eng, sem, val):
        if self.who == eng and val > 0:
            self.e.wait_ge(sem, val)

    def at(self, sem):
        return self.cnt.get(id(sem), 0)

    def dma(self, eng, fn, sem):
        self.do(eng, fn, inc=(sem, 16))
        return (sem, self.at(sem))

    def wt(self, eng, mark):
        if mark is not None:
            self.wait(eng, mark[0], mark[1])


def build_program():
    nc = bass.Bass("TRN2", target_bir_lowering=False)

    xb = nc.dram_tensor("xb", [S, C], F32, kind="ExternalInput").ap()
    wqkv = nc.dram_tensor("wqkv", [C, 3 * DQ], F32, kind="ExternalInput").ap()
    bq_c = nc.dram_tensor("bq_c", [P, 24], F32, kind="ExternalInput").ap()
    bq_r = nc.dram_tensor("bq_r", [1, 3 * DQ], F32, kind="ExternalInput").ap()
    wo = nc.dram_tensor("wo", [DQ, C], F32, kind="ExternalInput").ap()
    bo_r = nc.dram_tensor("bo_r", [1, C], F32, kind="ExternalInput").ap()
    qtg = nc.dram_tensor("qtg", [II, C], I32, kind="ExternalInput").ap()
    stg = nc.dram_tensor("stg", [II, NGU], F32, kind="ExternalInput").ap()
    ztg = nc.dram_tensor("ztg", [II, NGU], F32, kind="ExternalInput").ap()
    qtu = nc.dram_tensor("qtu", [II, C], I32, kind="ExternalInput").ap()
    stu = nc.dram_tensor("stu", [II, NGU], F32, kind="ExternalInput").ap()
    ztu = nc.dram_tensor("ztu", [II, NGU], F32, kind="ExternalInput").ap()
    qtd = nc.dram_tensor("qtd", [C, II], I32, kind="ExternalInput").ap()
    std = nc.dram_tensor("std", [C, NGD], F32, kind="ExternalInput").ap()
    ztd = nc.dram_tensor("ztd", [C, NGD], F32, kind="ExternalInput").ap()
    ident_d = nc.dram_tensor("ident", [P, P], BF16, kind="ExternalInput").ap()
    ones_d = nc.dram_tensor("ones", [1, P], BF16, kind="ExternalInput").ap()
    out_d = nc.dram_tensor("out", [T, C], F32, kind="ExternalOutput").ap()

    xnt_h = nc.dram_tensor("xnt_h", [NCT, P, S], BF16, kind="Internal").ap()
    v_h = nc.dram_tensor("v_h", [NST, P, DQ], BF16, kind="Internal").ap()
    sat_h = nc.dram_tensor("sat_h", [T, C], BF16, kind="Internal").ap()
    h_h = nc.dram_tensor("h_h", [NTT, P, NIT * P], BF16, kind="Internal").ap()

    base = {}

    with ExitStack() as root:
        R1 = root.enter_context(nc.sbuf_tensor("R1", [P, 44032], BF16))
        R2 = root.enter_context(nc.sbuf_tensor("R2", [P, 44032], BF16))
        ident = root.enter_context(nc.sbuf_tensor("ident_sb", [P, P], BF16))
        ones = root.enter_context(nc.sbuf_tensor("ones_sb", [1, P], BF16))
        bq_sb = root.enter_context(nc.sbuf_tensor("bq_sb", [P, 24], F32))

        ps_sc = root.enter_context(nc.psum_tensor("ps_sc", [P, 2048], F32))
        ps_ta = root.enter_context(nc.psum_tensor("ps_ta", [P, 1024], BF16))
        ps_tb = root.enter_context(nc.psum_tensor("ps_tb", [P, 1024], BF16))
        ps_aa = root.enter_context(nc.psum_tensor("ps_aa", [P, 512], F32))
        ps_ab = root.enter_context(nc.psum_tensor("ps_ab", [P, 512], F32))

        def sem(name):
            return root.enter_context(nc.semaphore(name))

        eP, eV, eA, BAR = sem("eP"), sem("eV"), sem("eA"), sem("BAR")
        dma_sems = []

        def dsem(name, eng):
            s = sem(name)
            dma_sems.append((s, eng))
            return s

        qL = dsem("qL", "sync")
        qG = dsem("qG", "gpsimd")
        aX = [dsem(f"aX{i}", "sync") for i in range(2)]
        sS = [dsem(f"sS{i}", "sync") for i in range(4)]
        bX = [dsem(f"bX{i}", "sync") for i in range(2)]
        bV = [dsem(f"bV{i}", "sync") for i in range(2)]
        dS = [dsem(f"dS{i}", "sync") for i in range(2)]
        eZ = [dsem(f"eZ{i}", "sync") for i in range(2)]
        eH = [dsem(f"eH{i}", "sync") for i in range(2)]
        eC = [dsem(f"eC{i}", "gpsimd") for i in range(2)]
        fH = [dsem(f"fH{i}", "sync") for i in range(2)]
        fC = [dsem(f"fC{i}", "sync") for i in range(2)]
        fO = [dsem(f"fO{i}", "sync") for i in range(2)]
        fQ = [[dsem(f"fQ{i}_{p}", "gpsimd") for p in range(4)] for i in range(3)]
        fZ = [dsem(f"fZ{i}", "sync") for i in range(3)]
        fH2 = [dsem(f"fH2_{i}", "sync") for i in range(2)]

        def barrier(em):
            tgt = em.at(BAR) + 5
            for eng in ENGS:
                for s, owner in dma_sems:
                    if owner == eng:
                        em.wait(eng, s, em.at(s))
                em.do(eng, lambda e: e.nop(), inc=(BAR, 1))
            for eng in ENGS:
                em.wait(eng, BAR, tgt)

        def run_phases(phases, alloc=None):
            with ExitStack() as ph:
                bufs = alloc(ph) if alloc is not None else None
                for nm, fn in phases:
                    with nc.Block(nm) as bl:
                        for eng in ENGS:
                            def mk(eng=eng, fn=fn):
                                def body(e):
                                    em = Em(eng, e, dict(base))
                                    fn(em, bufs)
                                    barrier(em)
                                return body
                            getattr(bl, eng)(mk())
                    em = Em(None, None, base)
                    fn(em, bufs)
                    barrier(em)

        def ph_init(em, _):
            em.dma("sync", lambda e: e.dma_start(out=ident[:], in_=ident_d), qL)
            em.dma("sync", lambda e: e.dma_start(out=ones[:], in_=ones_d), qL)
            em.dma("sync", lambda e: e.dma_start(out=bq_sb[:], in_=bq_c), qL)

        run_phases([("init", ph_init)])

        def alloc_A(ph):
            b = {}
            b["xin"] = [ph.enter_context(nc.sbuf_tensor(f"xin{i}", [P, C], F32)) for i in range(2)]
            b["stats"] = ph.enter_context(nc.sbuf_tensor("stats", [P, 48], F32))
            b["st1"] = ph.enter_context(nc.sbuf_tensor("st1", [P, 8], F32))
            return b

        base_vals = {}

        def load_w(em, col0, col1, dst, doff):
            mks = []
            w = col1 - col0
            for ct in range(NCT):
                mks.append(em.dma("gpsimd", lambda e, ct=ct, col0=col0, col1=col1, w=w, dst=dst, doff=doff: e.dma_start(
                    out=dst[:, doff + ct * w: doff + (ct + 1) * w],
                    in_=wqkv[ct * P:(ct + 1) * P, col0:col1]), qG))
            return mks

        def load_xslab(em, buf, sb):
            return em.dma("sync", lambda e, buf=buf, sb=sb: e.dma_start(
                out=buf.rearrange("p (a s) -> p a s", a=NCT),
                in_=xnt_h[:, :, sb * 256:(sb + 1) * 256].rearrange("a p s -> p a s")), bX[sb % 2])

        def ph_AV(em, b):
            xin, st1, stats = b["xin"], b["st1"], b["stats"]
            xnbf = R2[:, 0:4096]
            slab = [R2[:, 4096:8192], R2[:, 8192:12288]]
            vstg = [R2[:, 12288:13312], R2[:, 13312:14336]]
            bvrow = R2[0:1, 14336:15360]
            agg = st1[:, 0:2]
            var = st1[:, 1:2]
            nmean, sig, rstd = (st1[:, i:i + 1] for i in range(2, 5))
            bv_mark = em.dma("gpsimd", lambda e, bvrow=bvrow: e.dma_start(out=bvrow, in_=bq_r[:, 2 * DQ:3 * DQ]), qG)
            wv_marks = load_w(em, 2 * DQ, 3 * DQ, R1, 0)
            for ct in range(28):
                base_vals["wkpre_mark"] = em.dma("gpsimd", lambda e, ct=ct: e.dma_start(
                    out=R2[:, 15360 + ct * 1024: 15360 + (ct + 1) * 1024],
                    in_=wqkv[ct * P:(ct + 1) * P, DQ:2 * DQ]), qG)
            lx = [em.dma("sync", lambda e: e.dma_start(out=xin[0][:], in_=xb[0:P, :]), aX[0])]
            v_done = []
            s_evac = []
            ss = []
            p_done = []
            vmm_done = []
            vevac = []
            vs = []

            def emit_vevac(em, t):
                em.wait("vector", eP, vmm_done[t])
                if t >= 2:
                    em.wt("vector", vs[t - 2])
                pb = (t % 2) * 1024
                em.do("vector", lambda e, t=t, pb=pb: e.tensor_copy(
                    out=vstg[t % 2][:, 0:512], in_=ps_sc[:, pb:pb + 512]))
                em.do("vector", lambda e, t=t, pb=pb: e.tensor_copy(
                    out=vstg[t % 2][:, 512:1024], in_=ps_sc[:, pb + 512:pb + 1024]), inc=(eV, 1))
                vevac.append(em.at(eV))
                em.wait("sync", eV, vevac[t])
                vs.append(em.dma("sync", lambda e, t=t: e.dma_start(
                    out=v_h[t], in_=vstg[t % 2]), bV[t % 2]))

            for st in range(NST):
                if st + 1 < NST:
                    if st >= 1:
                        em.wait("sync", eV, v_done[st - 1])
                    lx.append(em.dma("sync", lambda e, st=st: e.dma_start(
                        out=xin[(st + 1) % 2][:], in_=xb[(st + 1) * P:(st + 2) * P, :]), aX[(st + 1) % 2]))
                cur = xin[st % 2]
                em.wt("vector", lx[st])
                for i in range(8):
                    em.do("vector", lambda e, cur=cur, i=i: e.bn_stats(
                        out=stats[:, 6 * i:6 * (i + 1)], in_=cur[:, 512 * i:512 * (i + 1)]),
                        inc=(eV, 1) if i == 7 else None)
                em.wait("vector", eV, em.at(eV))
                em.do("vector", lambda e: e.bn_aggr(out=agg, in_=stats[:]), inc=(eV, 1))
                em.wait("vector", eV, em.at(eV))
                em.do("vector", lambda e: e.tensor_scalar(
                    out=nmean, in0=agg[:, 0:1], scalar1=-1.0, scalar2=None, op0=OP.mult))
                em.do("vector", lambda e: e.tensor_scalar(
                    out=var, in0=var, scalar1=EPS, scalar2=None, op0=OP.add), inc=(eV, 1))
                var_done = em.at(eV)
                em.wait("scalar", eV, var_done)
                em.do("scalar", lambda e: e.activation(out=sig, in_=var, func=AF.Sqrt), inc=(eA, 1))
                sig_done = em.at(eA)
                em.wait("vector", eA, sig_done)
                em.do("vector", lambda e: e.reciprocal(out=rstd, in_=sig), inc=(eV, 1))
                em.wait("vector", eV, em.at(eV))
                if st >= 1:
                    em.wait("vector", eP, p_done[4 * (st - 1) + 3])
                em.do("vector", lambda e, cur=cur: e.tensor_scalar(
                    out=xnbf, in0=cur[:], scalar1=nmean, scalar2=rstd, op0=OP.add, op1=OP.mult), inc=(eV, 1))
                v_done.append(em.at(eV))
                if st >= 1:
                    emit_vevac(em, st - 1)
                em.wait("tensor", eV, v_done[st])
                for k in range(4):
                    gk = st * 4 + k
                    bank = ps_ta if gk % 2 == 0 else ps_tb
                    if gk >= 2:
                        em.wait("tensor", eA, s_evac[gk - 2])
                    for j in range(8):
                        ct = k * 8 + j
                        em.do("tensor", lambda e, ct=ct, bank=bank, j=j: e.transpose(
                            bank[:, j * 128:(j + 1) * 128], xnbf[:, ct * 128:(ct + 1) * 128], ident[:]),
                            inc=(eP, 1) if j == 7 else None)
                    p_done.append(em.at(eP))
                    em.wait("scalar", eP, p_done[gk])
                    if st >= 2:
                        em.wt("scalar", ss[4 * (st - 2) + k])
                        if k == 0:
                            em.wait("scalar", eP, vmm_done[st - 2])
                    em.do("scalar", lambda e, bank=bank, st=st, k=k: e.copy(
                        out=slab[st % 2][:, k * 1024:(k + 1) * 1024], in_=bank[:]), inc=(eA, 1))
                    s_evac.append(em.at(eA))
                    em.wait("sync", eA, s_evac[gk])
                    ss.append(em.dma("sync", lambda e, st=st, k=k: e.dma_start(
                        out=xnt_h[k * 8:(k + 1) * 8, :, st * P:(st + 1) * P].rearrange("a p s -> p a s"),
                        in_=slab[st % 2][:, k * 1024:(k + 1) * 1024].rearrange("p (a s) -> p a s", a=8)), sS[k]))
                if st == 0:
                    em.wt("tensor", bv_mark)
                if st >= 2:
                    em.wait("tensor", eV, vevac[st - 2])
                pb = (st % 2) * 1024
                for ct in range(NCT):
                    if ct % 8 == 0:
                        em.wait("tensor", eA, s_evac[st * 4 + ct // 8])
                        if st == 0:
                            em.wt("tensor", wv_marks[min(ct + 7, NCT - 1)])
                    for dvh in range(2):
                        em.do("tensor", lambda e, ct=ct, st=st, dvh=dvh, pb=pb: e.matmul(
                            ps_sc[:, pb + dvh * 512: pb + (dvh + 1) * 512],
                            slab[st % 2][:, ct * 128:(ct + 1) * 128],
                            R1[:, ct * DQ + dvh * 512: ct * DQ + (dvh + 1) * 512],
                            start=(ct == 0), stop=False))
                for dvh in range(2):
                    em.do("tensor", lambda e, dvh=dvh, pb=pb, bvrow=bvrow: e.matmul(
                        ps_sc[:, pb + dvh * 512: pb + (dvh + 1) * 512], ones[:],
                        bvrow[:, dvh * 512:(dvh + 1) * 512], start=False, stop=True),
                        inc=(eP, 1) if dvh == 1 else None)
                vmm_done.append(em.at(eP))
            emit_vevac(em, NST - 1)

        run_phases([("lnTV", ph_AV)], alloc_A)

        def alloc_Bk(ph):
            b = {}
            b["xslab"] = [ph.enter_context(nc.sbuf_tensor(f"kxsl{i}", [P, NCT * 256], BF16)) for i in range(2)]
            return b

        def ph_Bk(em, b):
            xslab = b["xslab"]

            def wk_ap(ct, o0, o1):
                if ct < 28:
                    return R2[:, 15360 + ct * 1024 + o0: 15360 + ct * 1024 + o1]
                return R2[:, (ct - 28) * 1024 + o0: (ct - 28) * 1024 + o1]

            wk_tail = None
            for ct in range(28, NCT):
                wk_tail = em.dma("gpsimd", lambda e, ct=ct: e.dma_start(
                    out=R2[:, (ct - 28) * 1024:(ct - 27) * 1024],
                    in_=wqkv[ct * P:(ct + 1) * P, DQ:2 * DQ]), qG)
            lx = [load_xslab(em, xslab[0][:], 0)]
            em.wt("tensor", wk_tail)
            em.wt("tensor", base_vals["wkpre_mark"])
            mm_done = []
            k_evac = []
            gidx = 0
            for sb in range(16):
                if sb + 1 < 16:
                    if sb >= 1:
                        em.wait("sync", eP, mm_done[8 * (sb - 1) + 7])
                    lx.append(load_xslab(em, xslab[(sb + 1) % 2][:], sb + 1))
                em.wt("tensor", lx[sb])
                for dk in range(NDQ):
                    ps = ps_aa if gidx % 2 == 0 else ps_ab
                    if gidx >= 2:
                        em.wait("tensor", eV, k_evac[gidx - 2])
                    for ct in range(NCT):
                        em.do("tensor", lambda e, ct=ct, dk=dk, sb=sb, ps=ps: e.matmul(
                            ps[:, 0:256], wk_ap(ct, dk * P, (dk + 1) * P),
                            xslab[sb % 2][:, ct * 256:(ct + 1) * 256],
                            start=(ct == 0), stop=(ct == NCT - 1)), inc=(eP, 1) if ct == NCT - 1 else None)
                    mm_done.append(em.at(eP))
                    em.wait("vector", eP, mm_done[gidx])
                    em.do("vector", lambda e, dk=dk, sb=sb, ps=ps: e.tensor_scalar(
                        out=R1[:, dk * S + sb * 256: dk * S + (sb + 1) * 256], in0=ps[:, 0:256],
                        scalar1=bq_sb[:, 8 + dk: 9 + dk], scalar2=None, op0=OP.add), inc=(eV, 1))
                    k_evac.append(em.at(eV))
                    gidx += 1

        run_phases([("projK", ph_Bk)], alloc_Bk)

        def alloc_C(ph):
            b = {}
            b["qT"] = ph.enter_context(nc.sbuf_tensor("qTb", [P, NDQ * T], BF16))
            b["Pb"] = ph.enter_context(nc.sbuf_tensor("Pbb", [P, 2 * S], BF16))
            b["at"] = ph.enter_context(nc.sbuf_tensor("atb", [P, DQ], BF16))
            b["sm"] = ph.enter_context(nc.sbuf_tensor("smb", [P, 32], F32))
            return b

        def ph_Bq(em, b):
            qT = b["qT"]
            xqs = [R2[:, 32768:40960], R1[:, 32768:40960]]
            wq_mark = load_w(em, 0, DQ, R2, 0)[-1]

            def load_xq(em, sb):
                return em.dma("sync", lambda e, sb=sb: e.dma_start(
                    out=xqs[sb % 2].rearrange("p (a s) -> p a s", a=NCT),
                    in_=xnt_h[:, :, sb * 256:(sb + 1) * 256].rearrange("a p s -> p a s")), qL)

            mm_done = []
            q_evac = []
            lx = [load_xq(em, 0)]
            gidx = 0
            for sb in range(4):
                if sb + 1 < 4:
                    if sb >= 1:
                        em.wait("sync", eP, mm_done[8 * (sb - 1) + 7])
                    lx.append(load_xq(em, sb + 1))
                em.wt("tensor", lx[sb])
                if sb == 0:
                    em.wt("tensor", wq_mark)
                for dk in range(NDQ):
                    ps = ps_aa if gidx % 2 == 0 else ps_ab
                    if gidx >= 2:
                        em.wait("tensor", eV, q_evac[gidx - 2])
                    for ct in range(NCT):
                        em.do("tensor", lambda e, ct=ct, dk=dk, ps=ps, sb=sb: e.matmul(
                            ps[:, 0:256], R2[:, ct * DQ + dk * P: ct * DQ + (dk + 1) * P],
                            xqs[sb % 2][:, ct * 256:(ct + 1) * 256],
                            start=(ct == 0), stop=(ct == NCT - 1)), inc=(eP, 1) if ct == NCT - 1 else None)
                    mm_done.append(em.at(eP))
                    em.wait("vector", eP, mm_done[gidx])
                    em.do("vector", lambda e, dk=dk, sb=sb, ps=ps: e.tensor_scalar(
                        out=qT[:, dk * T + sb * 256: dk * T + (sb + 1) * 256], in0=ps[:, 0:256],
                        scalar1=bq_sb[:, dk: dk + 1], scalar2=None, op0=OP.add), inc=(eV, 1))
                    q_evac.append(em.at(eV))
                    gidx += 1

        def ph_C(em, b):
            qT, Pb, at, sm = b["qT"], b["Pb"], b["at"], b["sm"]
            attnT = R1[:, 32768:40960]
            v_mark = em.dma("sync", lambda e: e.dma_start(
                out=R2[:, 0:NST * DQ].rearrange("p (a d) -> p a d", a=NST),
                in_=v_h.rearrange("a p d -> p a d")), qL)
            exp_done = {}
            pt_evac = []
            amm_done = {}
            at_evac = {}
            att_done = {}
            attT_evac = {}
            pfin = {}

            def smv(qt, i):
                blk = 16 * (qt % 2)
                return sm[:, blk + i: blk + i + 1]

            def pbuf(qt):
                return Pb[:, (qt % 2) * S:(qt % 2 + 1) * S]

            def emit_scores(em, qt, half):
                if (qt, half) != (0, 0):
                    prev = (qt, 0) if half == 1 else (qt - 1, 1)
                    em.wait("tensor", eA, exp_done[prev])
                for sbl in range(4):
                    sb = half * 4 + sbl
                    for dk in range(NDQ):
                        em.do("tensor", lambda e, sbl=sbl, sb=sb, dk=dk, qt=qt: e.matmul(
                            ps_sc[:, sbl * 512:(sbl + 1) * 512],
                            qT[:, dk * T + qt * P: dk * T + (qt + 1) * P],
                            R1[:, dk * S + sb * 512: dk * S + (sb + 1) * 512],
                            start=(dk == 0), stop=(dk == NDQ - 1)),
                            inc=(eP, 1) if (dk == NDQ - 1 and sbl == 3) else None)
                return em.at(eP)

            def emit_softmax_h(em, qt, half, sc_done):
                mdst = smv(qt, half)
                nm = smv(qt, 8 + half)
                sdst = smv(qt, 4 + half)
                em.wait("vector", eP, sc_done)
                em.do("vector", lambda e, mdst=mdst: e.reduce_max(out=mdst, in_=ps_sc[:], axis=AX), inc=(eV, 1))
                em.wait("vector", eV, em.at(eV))
                em.do("vector", lambda e, mdst=mdst, nm=nm: e.tensor_scalar(
                    out=nm, in0=mdst, scalar1=-SCL, scalar2=None, op0=OP.mult), inc=(eV, 1))
                em.wait("scalar", eV, em.at(eV))
                if qt >= 2 and half == 0:
                    em.wait("scalar", eP, amm_done[qt - 2])
                em.do("scalar", lambda e, qt=qt, half=half, nm=nm: e.activation(
                    out=pbuf(qt)[:, half * 2048:(half + 1) * 2048], in_=ps_sc[:],
                    func=AF.Exp, bias=nm, scale=SCL), inc=(eA, 1))
                exp_done[(qt, half)] = em.at(eA)
                em.wait("vector", eA, exp_done[(qt, half)])
                em.do("vector", lambda e, qt=qt, half=half, sdst=sdst: e.reduce_sum(
                    out=sdst, in_=pbuf(qt)[:, half * 2048:(half + 1) * 2048], axis=AX), inc=(eV, 1))
                em.wait("vector", eV, em.at(eV))

            def emit_merge(em, qt):
                m0, m1, mx, l_ = smv(qt, 0), smv(qt, 1), smv(qt, 2), smv(qt, 3)
                t0_, t1_, c0, c1 = smv(qt, 4), smv(qt, 5), smv(qt, 6), smv(qt, 7)
                linv = smv(qt, 10)
                em.do("vector", lambda e, m0=m0, m1=m1, mx=mx: e.tensor_tensor(
                    out=mx, in0=m0, in1=m1, op=OP.max), inc=(eV, 1))
                em.wait("vector", eV, em.at(eV))
                em.do("vector", lambda e, m0=m0, mx=mx, c0=c0: e.tensor_tensor(
                    out=c0, in0=m0, in1=mx, op=OP.subtract))
                em.do("vector", lambda e, m1=m1, mx=mx, c1=c1: e.tensor_tensor(
                    out=c1, in0=m1, in1=mx, op=OP.subtract), inc=(eV, 1))
                em.wait("scalar", eV, em.at(eV))
                em.do("scalar", lambda e, c0=c0: e.activation(out=c0, in_=c0, func=AF.Exp, scale=SCL))
                em.do("scalar", lambda e, c1=c1: e.activation(out=c1, in_=c1, func=AF.Exp, scale=SCL), inc=(eA, 1))
                em.wait("vector", eA, em.at(eA))
                em.do("vector", lambda e, t0_=t0_, c0=c0: e.tensor_scalar(
                    out=t0_, in0=t0_, scalar1=c0, scalar2=None, op0=OP.mult))
                em.do("vector", lambda e, t1_=t1_, c1=c1: e.tensor_scalar(
                    out=t1_, in0=t1_, scalar1=c1, scalar2=None, op0=OP.mult), inc=(eV, 1))
                em.wait("vector", eV, em.at(eV))
                em.do("vector", lambda e, l_=l_, t0_=t0_, t1_=t1_: e.tensor_tensor(
                    out=l_, in0=t0_, in1=t1_, op=OP.add), inc=(eV, 1))
                em.wait("vector", eV, em.at(eV))
                em.do("vector", lambda e, linv=linv, l_=l_: e.reciprocal(out=linv, in_=l_), inc=(eV, 1))
                em.wait("vector", eV, em.at(eV))
                em.do("vector", lambda e, qt=qt, c0=c0: e.tensor_scalar(
                    out=pbuf(qt)[:, 0:2048], in0=pbuf(qt)[:, 0:2048], scalar1=c0, scalar2=None, op0=OP.mult))
                em.do("vector", lambda e, qt=qt, c1=c1: e.tensor_scalar(
                    out=pbuf(qt)[:, 2048:4096], in0=pbuf(qt)[:, 2048:4096], scalar1=c1, scalar2=None, op0=OP.mult), inc=(eV, 1))
                pfin[qt] = em.at(eV)

            def emit_PT(em, qt):
                em.wait("tensor", eV, pfin[qt])
                if qt >= 1:
                    em.wait("tensor", eA, attT_evac[qt - 1])
                for k in range(4):
                    gk = qt * 4 + k
                    bank = ps_ta if k % 2 == 0 else ps_tb
                    if gk >= 2:
                        em.wait("tensor", eA, pt_evac[gk - 2])
                    for j in range(8):
                        sc = k * 8 + j
                        em.do("tensor", lambda e, qt=qt, sc=sc, j=j, bank=bank: e.transpose(
                            bank[:, j * 128:(j + 1) * 128], pbuf(qt)[:, sc * 128:(sc + 1) * 128], ident[:]),
                            inc=(eP, 1) if j == 7 else None)
                    tb_done = em.at(eP)
                    em.wait("scalar", eP, tb_done)
                    em.do("scalar", lambda e, qt=qt, bank=bank, k=k: e.copy(
                        out=pbuf(qt)[:, k * 1024:(k + 1) * 1024], in_=bank[:]), inc=(eA, 1))
                    pt_evac.append(em.at(eA))

            def emit_PV(em, qt):
                em.wait("tensor", eA, pt_evac[qt * 4 + 3])
                em.wt("tensor", v_mark)
                if qt >= 1:
                    em.wait("tensor", eV, at_evac[qt - 1])
                for half, ps in ((0, ps_aa), (1, ps_ab)):
                    for sc in range(NST):
                        em.do("tensor", lambda e, qt=qt, sc=sc, ps=ps, half=half: e.matmul(
                            ps[:], pbuf(qt)[:, sc * P:(sc + 1) * P],
                            R2[:, sc * DQ + half * 512: sc * DQ + (half + 1) * 512],
                            start=(sc == 0), stop=(sc == NST - 1)),
                            inc=(eP, 1) if (half == 1 and sc == NST - 1) else None)
                amm_done[qt] = em.at(eP)
                linv = smv(qt, 10)
                em.wait("vector", eP, amm_done[qt])
                if qt >= 1:
                    em.wait("vector", eP, att_done[qt - 1])
                em.do("vector", lambda e, linv=linv: e.tensor_scalar(
                    out=at[:, 0:512], in0=ps_aa[:], scalar1=linv, scalar2=None, op0=OP.mult))
                em.do("vector", lambda e, linv=linv: e.tensor_scalar(
                    out=at[:, 512:1024], in0=ps_ab[:], scalar1=linv, scalar2=None, op0=OP.mult), inc=(eV, 1))
                at_evac[qt] = em.at(eV)
                em.wait("tensor", eV, at_evac[qt])
                for j in range(NDQ):
                    em.do("tensor", lambda e, j=j: e.transpose(
                        ps_ta[:, j * 128:(j + 1) * 128], at[:, j * 128:(j + 1) * 128], ident[:]),
                        inc=(eP, 1) if j == NDQ - 1 else None)
                att_done[qt] = em.at(eP)
                em.wait("scalar", eP, att_done[qt])
                em.do("scalar", lambda e, qt=qt, attnT=attnT: e.copy(
                    out=attnT.rearrange("p (a t) -> p a t", a=NDQ)[:, :, qt * P:(qt + 1) * P],
                    in_=ps_ta[:].rearrange("p (a t) -> p a t", a=NDQ)), inc=(eA, 1))
                attT_evac[qt] = em.at(eA)

            for qt in range(NTT):
                scd0 = emit_scores(em, qt, 0)
                emit_softmax_h(em, qt, 0, scd0)
                if qt >= 1:
                    emit_PT(em, qt - 1)
                scd1 = emit_scores(em, qt, 1)
                emit_softmax_h(em, qt, 1, scd1)
                emit_merge(em, qt)
                if qt >= 1:
                    emit_PV(em, qt - 1)
            emit_PT(em, NTT - 1)
            emit_PV(em, NTT - 1)
            base_vals["amm_last"] = amm_done[NTT - 1]
            base_vals["exp_last"] = exp_done[(NTT - 1, 1)]

        run_phases([("projQ", ph_Bq), ("attnC", ph_C)], alloc_C)

        def ph_D(em, b):
            attnT = R1[:, 32768:40960]
            sat = [R2[:, 40960:41984], R2[:, 41984:43008]]
            bo_sb = R2[0:1, 36864:40960]
            em.wait("gpsimd", eP, base_vals["amm_last"])
            bo_mark = em.dma("gpsimd", lambda e, bo_sb=bo_sb: e.dma_start(out=bo_sb, in_=bo_r), qG)
            wo_marks = []
            for dvc in range(NDQ):
                wo_marks.append(em.dma("gpsimd", lambda e, dvc=dvc: e.dma_start(
                    out=R2[:, dvc * C:(dvc + 1) * C],
                    in_=wo[dvc * P:(dvc + 1) * P, :]), qG))
            for ct in range(NCT):
                base_vals["xnt_mark"] = em.dma("sync", lambda e, ct=ct: e.dma_start(
                    out=R1[:, ct * T:(ct + 1) * T], in_=xnt_h[ct, :, 0:T]), qL)
            em.wait("tensor", eA, base_vals["exp_last"])
            ds = []
            evac_done = []
            for tt in range(NTT):
                for qtr in range(4):
                    g = tt * 4 + qtr
                    half = qtr % 2
                    if g >= 2:
                        em.wait("tensor", eA, evac_done[g - 2])
                    for dvc in range(NDQ):
                        if g == 0:
                            em.wt("tensor", wo_marks[dvc])
                        for cbl in range(2):
                            cb = qtr * 2 + cbl
                            em.do("tensor", lambda e, half=half, cbl=cbl, cb=cb, dvc=dvc, tt=tt: e.matmul(
                                ps_sc[:, (half * 2 + cbl) * 512:(half * 2 + cbl + 1) * 512],
                                attnT[:, dvc * T + tt * P: dvc * T + (tt + 1) * P],
                                R2[:, dvc * C + cb * 512: dvc * C + (cb + 1) * 512],
                                start=(dvc == 0), stop=False))
                    if g == 0:
                        em.wt("tensor", bo_mark)
                    for cbl in range(2):
                        cb = qtr * 2 + cbl
                        em.do("tensor", lambda e, half=half, cbl=cbl, cb=cb, bo_sb=bo_sb: e.matmul(
                            ps_sc[:, (half * 2 + cbl) * 512:(half * 2 + cbl + 1) * 512], ones[:],
                            bo_sb[:, cb * 512:(cb + 1) * 512], start=False, stop=True),
                            inc=(eP, 1) if cbl == 1 else None)
                    mm_done = em.at(eP)
                    em.wait("scalar", eP, mm_done)
                    if g >= 2:
                        em.wt("scalar", ds[g - 2])
                    em.do("scalar", lambda e, half=half: e.copy(
                        out=sat[half], in_=ps_sc[:, half * 1024:(half + 1) * 1024]), inc=(eA, 1))
                    evac_done.append(em.at(eA))
                    em.wait("sync", eA, evac_done[g])
                    ds.append(em.dma("sync", lambda e, tt=tt, qtr=qtr, half=half: e.dma_start(
                        out=sat_h[tt * P:(tt + 1) * P, qtr * 1024:(qtr + 1) * 1024],
                        in_=sat[half]), dS[half]))

        run_phases([("projO", ph_D)])

        def alloc_E(ph):
            b = {}
            b["szg"] = [ph.enter_context(nc.sbuf_tensor(f"szg{i}", [P, 64], F32)) for i in range(2)]
            b["szu"] = [ph.enter_context(nc.sbuf_tensor(f"szu{i}", [P, 64], F32)) for i in range(2)]
            b["h"] = [ph.enter_context(nc.sbuf_tensor(f"hh{i}", [P, 512], BF16)) for i in range(2)]
            b["sg"] = ph.enter_context(nc.sbuf_tensor("sgg", [P, 1024], BF16))
            b["gu2"] = ph.enter_context(nc.sbuf_tensor("gu2", [P, 8192], BF16))
            return b

        def ph_E(em, b):
            szg, szu, hbuf, sgb = b["szg"], b["szu"], b["h"], b["sg"]
            gu2 = b["gu2"]
            xnt_mark = base_vals["xnt_mark"]
            QG0, QU0, WG0, WU0 = 0, 8192, 24576, 32768
            GDs = [(R2, 16384), (gu2, 0)]
            UDs = [(R2, 20480), (gu2, 4096)]
            deq_done = []
            tr_done = []
            tr_evac = []
            mm_done = []
            silu_done = []
            h_done = []
            hs = []
            cast_marks = []
            sz_marks = []

            def emit_dmas(em, it):
                if it >= 2:
                    em.wait("sync", eV, deq_done[2 * (it - 2) + 1])
                em.dma("sync", lambda e, it=it: e.dma_start(
                    out=szg[it % 2][:, 0:32], in_=stg[it * P:(it + 1) * P, :]), eZ[it % 2])
                em.dma("sync", lambda e, it=it: e.dma_start(
                    out=szg[it % 2][:, 32:64], in_=ztg[it * P:(it + 1) * P, :]), eZ[it % 2])
                em.dma("sync", lambda e, it=it: e.dma_start(
                    out=szu[it % 2][:, 0:32], in_=stu[it * P:(it + 1) * P, :]), eZ[it % 2])
                sz_marks.append(em.dma("sync", lambda e, it=it: e.dma_start(
                    out=szu[it % 2][:, 32:64], in_=ztu[it * P:(it + 1) * P, :]), eZ[it % 2]))
                if it >= 2:
                    em.wait("gpsimd", eV, deq_done[2 * (it - 2) + 1])
                em.dma("gpsimd", lambda e, it=it: e.dma_start(
                    out=R2[:, QG0 + (it % 2) * C: QG0 + (it % 2 + 1) * C],
                    in_=qtg[it * P:(it + 1) * P, :]), eC[it % 2])
                cast_marks.append(em.dma("gpsimd", lambda e, it=it: e.dma_start(
                    out=R2[:, QU0 + (it % 2) * C: QU0 + (it % 2 + 1) * C],
                    in_=qtu[it * P:(it + 1) * P, :]), eC[it % 2]))

            def emit_h(em, it, r):
                idx = 2 * it + r
                em.wait("vector", eA, silu_done[idx])
                if it >= 1:
                    em.wt("vector", hs[idx - 2])
                em.do("vector", lambda e, it=it, r=r: e.tensor_tensor(
                    out=hbuf[r][:], in0=sgb[:, r * 512:(r + 1) * 512],
                    in1=ps_sc[:, (2 * r + 1) * 512:(2 * r + 2) * 512], op=OP.mult), inc=(eV, 1))
                h_done.append(em.at(eV))
                em.wait("sync", eV, h_done[idx])
                mk = None
                for j in range(4):
                    mk = em.dma("sync", lambda e, it=it, r=r, j=j: e.dma_start(
                        out=h_h[4 * r + j, :, it * P:(it + 1) * P],
                        in_=hbuf[r][:, j * 128:(j + 1) * 128]), eH[r])
                hs.append(mk)

            def emit_deq(em, it):
                em.wt("vector", cast_marks[it])
                em.wt("vector", sz_marks[it])
                if it >= 2:
                    em.wait("vector", eP, tr_done[8 * (it - 2) + 7])
                em.do("vector", lambda e, it=it: e.scalar_tensor_tensor(
                    out=szg[it % 2][:, 32:64], in0=szg[it % 2][:, 0:32], scalar=-8.0,
                    in1=szg[it % 2][:, 32:64], op0=OP.mult, op1=OP.add))
                em.do("vector", lambda e, it=it: e.scalar_tensor_tensor(
                    out=szu[it % 2][:, 32:64], in0=szu[it % 2][:, 0:32], scalar=-8.0,
                    in1=szu[it % 2][:, 32:64], op0=OP.mult, op1=OP.add), inc=(eV, 1))
                em.wait("vector", eV, em.at(eV))
                for src0, dslot, szb in ((QG0, GDs[it % 2], szg), (QU0, UDs[it % 2], szu)):
                    dmat, dst0 = dslot
                    for g in range(NGU):
                        em.do("vector", lambda e, it=it, g=g, src0=src0, dmat=dmat, dst0=dst0, szb=szb: e.tensor_scalar(
                            out=dmat[:, dst0 + g * P: dst0 + (g + 1) * P],
                            in0=R2[:, src0 + (it % 2) * C + g * P: src0 + (it % 2) * C + (g + 1) * P],
                            scalar1=szb[it % 2][:, g:g + 1],
                            scalar2=szb[it % 2][:, 32 + g:33 + g],
                            op0=OP.mult, op1=OP.add), inc=(eV, 1) if g == NGU - 1 else None)
                    deq_done.append(em.at(eV))

            emit_dmas(em, 0)
            emit_deq(em, 0)
            for it in range(NIT):
                if it + 1 < NIT:
                    emit_dmas(em, it + 1)
                if it >= 1:
                    emit_h(em, it - 1, 1)
                for k in range(8):
                    pp, which = k // 2, k % 2
                    dmat, mat_off = (GDs if which == 0 else UDs)[it % 2]
                    src_deq = deq_done[2 * it + which]
                    em.wait("tensor", eV, src_deq)
                    gk = it * 8 + k
                    bank = ps_ta if k % 2 == 0 else ps_tb
                    if gk >= 2:
                        mk = tr_evac[gk - 2]
                        em.wait("tensor", eA if mk[0] == "A" else eV, mk[1])
                    for j in range(8):
                        ct = pp * 8 + j
                        em.do("tensor", lambda e, ct=ct, j=j, bank=bank, dmat=dmat, mat_off=mat_off: e.transpose(
                            bank[:, j * 128:(j + 1) * 128],
                            dmat[:, mat_off + ct * P: mat_off + (ct + 1) * P], ident[:]),
                            inc=(eP, 1) if j == 7 else None)
                    tr_done.append(em.at(eP))
                    wdst = WG0 if which == 0 else WU0
                    wdst0 = wdst + (it % 2) * C + pp * 1024
                    if k % 2 == 0:
                        em.wait("scalar", eP, tr_done[gk])
                        em.do("scalar", lambda e, bank=bank, wdst0=wdst0: e.copy(
                            out=R2[:, wdst0:wdst0 + 1024], in_=bank[:]), inc=(eA, 1))
                        tr_evac.append(("A", em.at(eA)))
                    else:
                        em.wait("vector", eP, tr_done[gk])
                        em.do("vector", lambda e, bank=bank, wdst0=wdst0: e.tensor_copy(
                            out=R2[:, wdst0:wdst0 + 1024], in_=bank[:]), inc=(eV, 1))
                        tr_evac.append(("V", em.at(eV)))
                if it + 1 < NIT:
                    emit_deq(em, it + 1)
                em.wt("tensor", xnt_mark)
                for r in range(2):
                    if it >= 1:
                        em.wait("tensor", eA, silu_done[2 * (it - 1) + r])
                        em.wait("tensor", eV, h_done[2 * (it - 1) + r])
                    for ct in range(NCT):
                        if r == 0 and ct % 8 == 0:
                            for mk in (tr_evac[it * 8 + 2 * (ct // 8)], tr_evac[it * 8 + 2 * (ct // 8) + 1]):
                                em.wait("tensor", eA if mk[0] == "A" else eV, mk[1])
                        for mi, woff in ((0, WG0), (1, WU0)):
                            em.do("tensor", lambda e, ct=ct, mi=mi, woff=woff, r=r, it=it: e.matmul(
                                ps_sc[:, (2 * r + mi) * 512:(2 * r + mi + 1) * 512],
                                R2[:, woff + (it % 2) * C + ct * P: woff + (it % 2) * C + (ct + 1) * P],
                                R1[:, ct * T + r * 512: ct * T + (r + 1) * 512],
                                start=(ct == 0), stop=(ct == NCT - 1)),
                                inc=(eP, 1) if (ct == NCT - 1 and mi == 1) else None)
                    mm_done.append(em.at(eP))
                    em.wait("scalar", eP, mm_done[2 * it + r])
                    if it >= 1:
                        em.wait("scalar", eV, h_done[2 * (it - 1) + r])
                    em.do("scalar", lambda e, r=r: e.activation(
                        out=sgb[:, r * 512:(r + 1) * 512],
                        in_=ps_sc[:, 2 * r * 512:(2 * r + 1) * 512], func=AF.Silu), inc=(eA, 1))
                    silu_done.append(em.at(eA))
                    if r == 0:
                        emit_h(em, it, 0)
            emit_h(em, NIT - 1, 1)

        run_phases([("ffnGU", ph_E)], alloc_E)

        def alloc_F(ph):
            b = {}
            b["qd"] = ph.enter_context(nc.sbuf_tensor("qd0", [P, II], BF16))
            b["szd"] = [ph.enter_context(nc.sbuf_tensor(f"szd{i}", [P, 2 * NGD], F32)) for i in range(3)]
            b["g"] = [ph.enter_context(nc.sbuf_tensor(f"gg{i}", [P, 512], BF16)) for i in range(2)]
            b["sgd"] = ph.enter_context(nc.sbuf_tensor("sgdd", [P, 512], F32))
            b["satl"] = [ph.enter_context(nc.sbuf_tensor(f"satl{i}", [P, 512], BF16)) for i in range(2)]
            b["xres"] = [ph.enter_context(nc.sbuf_tensor(f"xres{i}", [P, 512], F32)) for i in range(2)]
            return b

        def ph_F(em, b):
            szd, gbuf, sgd, satl, xres = (
                b["szd"], b["g"], b["sgd"], b["satl"], b["xres"])
            hbufs = [R2[:, 0:II], R2[:, II:2 * II]]
            qdbs = [(b["qd"], 0), (R2, 2 * II), (R2, 3 * II)]

            def qd_ap(k, c0, c1):
                t, off = qdbs[k % 3]
                return t[:, off + c0: off + c1]

            NB = (NGD + 7) // 8
            PIECES = [(0, 3072), (3072, 6144), (6144, 9216), (9216, II)]
            deq_banks = {}
            ch_copy = {}
            tr_done = []
            tr_evac = []
            dn_done = []
            silu_done = []
            comb_done = []
            h_marks = []
            c_marks = []
            o_marks = []
            qd_marks = {}

            def emit_chain_dma(em, k):
                d0 = k * P
                if k >= 3:
                    em.wait("sync", eV, deq_banks[k - 3][-1])
                em.dma("sync", lambda e, k=k, d0=d0: e.dma_start(
                    out=szd[k % 3][:, 0:NGD], in_=std[d0:d0 + P, :]), fZ[k % 3])
                em.dma("sync", lambda e, k=k, d0=d0: e.dma_start(
                    out=szd[k % 3][:, NGD:2 * NGD], in_=ztd[d0:d0 + P, :]), fZ[k % 3])
                marks = []
                for pi, (c0, c1) in enumerate(PIECES):
                    if k >= 3:
                        lb = min(((c1 + 1023) // 1024) - 1, NB - 1)
                        em.wait("gpsimd", eP, tr_done[NB * (k - 3) + lb])
                    marks.append(em.dma("gpsimd", lambda e, k=k, d0=d0, c0=c0, c1=c1: e.dma_start(
                        out=qd_ap(k, c0, c1), in_=qtd[d0:d0 + P, c0:c1]), fQ[k % 3][pi]))
                qd_marks[k] = marks

            def emit_chain_deq(em, k):
                sd_ = szd[k % 3][:, 0:NGD]
                z8d_ = szd[k % 3][:, NGD:2 * NGD]
                em.wait("vector", fZ[k % 3], (k // 3 + 1) * 32)
                em.do("vector", lambda e, sd_=sd_, z8d_=z8d_: e.scalar_tensor_tensor(
                    out=z8d_, in0=sd_, scalar=-8.0, in1=z8d_, op0=OP.mult, op1=OP.add), inc=(eV, 1))
                em.wait("vector", eV, em.at(eV))
                banks = []
                for kb in range(NB):
                    j0, j1 = kb * 8, min(kb * 8 + 8, NGD)
                    pi = next(i for i, (c0, c1) in enumerate(PIECES) if j1 * P <= c1)
                    em.wt("vector", qd_marks[k][pi])
                    for g in range(j0, j1):
                        em.do("vector", lambda e, k=k, g=g, sd_=sd_, z8d_=z8d_: e.tensor_scalar(
                            out=qd_ap(k, g * P, (g + 1) * P), in0=qd_ap(k, g * P, (g + 1) * P),
                            scalar1=sd_[:, g:g + 1], scalar2=z8d_[:, g:g + 1],
                            op0=OP.mult, op1=OP.add), inc=(eV, 1) if g == j1 - 1 else None)
                    banks.append(em.at(eV))
                deq_banks[k] = banks

            def emit_chain_trans(em, k):
                dt = k % 4
                for kb in range(NB):
                    j0 = kb * 8
                    j1 = min(j0 + 8, NGD)
                    em.wait("tensor", eV, deq_banks[k][kb])
                    bank = ps_ta if len(tr_done) % 2 == 0 else ps_tb
                    if len(tr_evac) >= 2:
                        em.wait("tensor", eA, tr_evac[-2])
                    for j in range(j0, j1):
                        em.do("tensor", lambda e, k=k, j=j, j0=j0, bank=bank: e.transpose(
                            bank[:, (j - j0) * 128:(j - j0 + 1) * 128],
                            qd_ap(k, j * P, (j + 1) * P), ident[:]),
                            inc=(eP, 1) if j == j1 - 1 else None)
                    tr_done.append(em.at(eP))
                    em.wait("scalar", eP, tr_done[-1])
                    w = (j1 - j0) * 128
                    em.do("scalar", lambda e, bank=bank, j0=j0, j1=j1, dt=dt, w=w: e.copy(
                        out=R1[:].rearrange("p (a t) -> p a t", a=NGD)[:, j0:j1, dt * P:(dt + 1) * P],
                        in_=bank[:, 0:w].rearrange("p (a t) -> p a t", a=j1 - j0)), inc=(eA, 1))
                    tr_evac.append(em.at(eA))
                ch_copy[k] = em.at(eA)

            HHALF = 43 * P

            def emit_hload(em, gti):
                if gti >= 2:
                    em.wait("sync", eP, dn_done[gti - 2])
                m0 = em.dma("sync", lambda e, gti=gti: e.dma_start(
                    out=hbufs[gti % 2][:, 0:HHALF], in_=h_h[gti % 8][:, 0:HHALF]), fH[gti % 2])
                m1 = em.dma("sync", lambda e, gti=gti: e.dma_start(
                    out=hbufs[gti % 2][:, HHALF:II], in_=h_h[gti % 8][:, HHALF:II]), fH2[gti % 2])
                h_marks.append((m0, m1))

            for k in range(3):
                emit_chain_dma(em, k)
            for k in range(3):
                emit_chain_deq(em, k)
            emit_chain_trans(em, 0)
            emit_chain_dma(em, 3)
            emit_chain_deq(em, 3)
            emit_chain_trans(em, 1)
            emit_chain_trans(em, 2)
            emit_chain_trans(em, 3)
            emit_hload(em, 0)
            emit_hload(em, 1)

            for db in range(8):
                for ti in range(8):
                    gti = db * 8 + ti
                    ps = ps_aa if ti % 2 == 0 else ps_ab
                    if ti == 0:
                        em.wait("tensor", eA, ch_copy[4 * db + 3])
                    em.wt("tensor", h_marks[gti][0])
                    if gti >= 2:
                        em.wait("tensor", eA, silu_done[gti - 2])
                    for a in range(NIT):
                        if a == 43:
                            em.wt("tensor", h_marks[gti][1])
                        em.do("tensor", lambda e, a=a, ti=ti, ps=ps: e.matmul(
                            ps[:], hbufs[gti % 2][:, a * P:(a + 1) * P],
                            R1[:, a * 512:(a + 1) * 512],
                            start=(a == 0), stop=(a == NIT - 1)), inc=(eP, 1) if a == NIT - 1 else None)
                    dn_done.append(em.at(eP))
                    if gti + 2 < 64:
                        emit_hload(em, gti + 2)
                    if len(comb_done) >= 2:
                        em.wait("sync", eV, comb_done[-2])
                    if len(o_marks) >= 2:
                        em.wt("sync", o_marks[-2])
                    em.dma("sync", lambda e, ti=ti, db=db: e.dma_start(
                        out=satl[ti % 2][:], in_=sat_h[ti * P:(ti + 1) * P, db * 512:(db + 1) * 512]), fC[ti % 2])
                    c_marks.append(em.dma("sync", lambda e, ti=ti, db=db: e.dma_start(
                        out=xres[ti % 2][:], in_=xb[ti * P:(ti + 1) * P, db * 512:(db + 1) * 512]), fC[ti % 2]))
                    em.wait("scalar", eP, dn_done[-1])
                    if gti >= 2:
                        em.wait("scalar", eV, comb_done[gti - 2])
                    em.do("scalar", lambda e, ti=ti, ps=ps: e.activation(
                        out=gbuf[ti % 2][:], in_=ps[:], func=AF.Silu), inc=(eA, 1))
                    silu_done.append(em.at(eA))
                    em.wait("vector", eA, silu_done[-1])
                    em.wt("vector", c_marks[-1])
                    if len(o_marks) >= 2:
                        em.wt("vector", o_marks[-2])
                    em.do("vector", lambda e, ti=ti: e.tensor_tensor(
                        out=gbuf[ti % 2][:], in0=satl[ti % 2][:], in1=gbuf[ti % 2][:], op=OP.mult), inc=(eV, 1))
                    em.wait("vector", eV, em.at(eV))
                    em.do("vector", lambda e, ti=ti: e.tensor_copy(out=sgd[:], in_=gbuf[ti % 2][:]), inc=(eV, 1))
                    em.wait("vector", eV, em.at(eV))
                    em.do("vector", lambda e, ti=ti: e.tensor_tensor(
                        out=xres[ti % 2][:], in0=xres[ti % 2][:], in1=sgd[:], op=OP.add), inc=(eV, 1))
                    comb_done.append(em.at(eV))
                    em.wait("sync", eV, comb_done[-1])
                    o_marks.append(em.dma("sync", lambda e, ti=ti, db=db: e.dma_start(
                        out=out_d[ti * P:(ti + 1) * P, db * 512:(db + 1) * 512],
                        in_=xres[ti % 2][:]), fO[ti % 2]))
                    if db < 7 and ti < 3:
                        kk = 4 * (db + 1) + ti
                        emit_chain_dma(em, kk)
                        emit_chain_deq(em, kk)
                if db < 7:
                    emit_chain_trans(em, 4 * (db + 1))
                    emit_chain_dma(em, 4 * (db + 1) + 3)
                    emit_chain_trans(em, 4 * (db + 1) + 1)
                    emit_chain_trans(em, 4 * (db + 1) + 2)
                    emit_chain_deq(em, 4 * (db + 1) + 3)
                    emit_chain_trans(em, 4 * (db + 1) + 3)

        run_phases([("ffnDN", ph_F)], alloc_F)

    return nc


_NC_CACHE = {}


def kernel(L0_attn_x, L0_attn_Wqkv, L0_attn_bqkv, L0_attn_Wo, L0_attn_bo,
           L0_ffn_gate_q, L0_ffn_gate_s, L0_ffn_gate_z,
           L0_ffn_up_q, L0_ffn_up_s, L0_ffn_up_z,
           L0_ffn_down_q, L0_ffn_down_s, L0_ffn_down_z):
    x = np.asarray(L0_attn_x, dtype=np.float32)
    wqkv = np.ascontiguousarray(np.asarray(L0_attn_Wqkv, dtype=np.float32))
    bqkv = np.asarray(L0_attn_bqkv, dtype=np.float32)
    wo_np = np.ascontiguousarray(np.asarray(L0_attn_Wo, dtype=np.float32))
    bo = np.asarray(L0_attn_bo, dtype=np.float32)
    qtg = np.ascontiguousarray(np.asarray(L0_ffn_gate_q, dtype=np.int32).T)
    stg = np.ascontiguousarray(np.asarray(L0_ffn_gate_s, dtype=np.float32).T)
    ztg = np.ascontiguousarray(np.asarray(L0_ffn_gate_z, dtype=np.float32).T)
    qtu = np.ascontiguousarray(np.asarray(L0_ffn_up_q, dtype=np.int32).T)
    stu = np.ascontiguousarray(np.asarray(L0_ffn_up_s, dtype=np.float32).T)
    ztu = np.ascontiguousarray(np.asarray(L0_ffn_up_z, dtype=np.float32).T)
    qtd = np.ascontiguousarray(np.asarray(L0_ffn_down_q, dtype=np.int32).T)
    std = np.ascontiguousarray(np.asarray(L0_ffn_down_s, dtype=np.float32).T)
    ztd = np.ascontiguousarray(np.asarray(L0_ffn_down_z, dtype=np.float32).T)

    bq_c = np.ascontiguousarray(bqkv.reshape(24, P).T)
    bq_r = np.ascontiguousarray(bqkv.reshape(1, 3 * DQ))
    bo_r = np.ascontiguousarray(bo.reshape(1, C))
    ident = np.eye(P, dtype=ml_dtypes.bfloat16)
    ones = np.ones((1, P), dtype=ml_dtypes.bfloat16)

    if "nc" not in _NC_CACHE:
        _NC_CACHE["nc"] = build_program()
    nc = _NC_CACHE["nc"]

    in_maps = []
    for c in range(8):
        bi = c // 4
        t0 = (c % 4) * T
        xb_c = np.ascontiguousarray(np.roll(x[bi], -t0, axis=0))
        in_maps.append(dict(
            xb=xb_c, wqkv=wqkv, bq_c=bq_c, bq_r=bq_r, wo=wo_np, bo_r=bo_r,
            qtg=qtg, stg=stg, ztg=ztg, qtu=qtu, stu=stu, ztu=ztu,
            qtd=qtd, std=std, ztd=ztd, ident=ident, ones=ones,
        ))
    res = run_bass_kernel_spmd(nc, in_maps, core_ids=list(range(8)))
    out = np.empty((2, S, C), dtype=np.float32)
    for c in range(8):
        bi = c // 4
        t0 = (c % 4) * T
        out[bi, t0:t0 + T, :] = res.results[c]["out"]
    return out


if __name__ == "__main__":
    nc = build_program()
    n = sum(1 for _ in nc.all_instructions())
    print("built, instructions:", n)
